# version 05
# baseline (speedup 1.0000x reference)
import numpy as np
import ml_dtypes
from contextlib import ExitStack

import concourse.bass as bass
import concourse.mybir as mybir
from concourse.bass_utils import run_bass_kernel_spmd

F32 = mybir.dt.float32
BF16 = mybir.dt.bfloat16
AF = mybir.ActivationFunctionType
ALU = mybir.AluOpType
AX = mybir.AxisListType

B, L, D = 4, 4096, 1024
DI = 2048
NST = 16
DTR = 64
DH = 4096
E, DFF = 8, 4096
LH = L // 2
DIH = DI // 2
EPS = 1e-5


class Prog:

    def __init__(self):
        self.nc = bass.Bass("TRN2", target_bir_lowering=False)
        self.stack = ExitStack()
        self.sems = {}

    def sem(self, name):
        h = self.stack.enter_context(self.nc.semaphore(name))
        self.sems[name] = [h, 0]
        return h

    def inc(self, instr, name, amt=1):
        s = self.sems[name]
        instr.then_inc(s[0], amt)
        s[1] += amt
        return instr

    def cnt(self, name):
        return self.sems[name][1]

    def wait(self, eng, name, cnt=None):
        s = self.sems[name]
        eng.wait_ge(s[0], s[1] if cnt is None else cnt)

    def sbuf(self, name, shape, dt):
        return self.stack.enter_context(self.nc.sbuf_tensor(name, shape, dt)).ap()

    def psum(self, name, shape, dt=F32):
        return self.stack.enter_context(self.nc.psum_tensor(name, shape, dt)).ap()

    def din(self, name, shape, dt=F32):
        return self.nc.dram_tensor(name, shape, dt, kind="ExternalInput").ap()

    def dout(self, name, shape, dt=F32):
        return self.nc.dram_tensor(name, shape, dt, kind="ExternalOutput").ap()

    def close(self):
        self.stack.close()

    def init_fences(self, pool=False):
        self.f_dv = self.sbuf("fence_dv", [1, 128], F32)
        self.f_sc = self.f_dv
        self.f_po = self.f_dv

    def fence_dv(self):
        self.inc(self.nc.vector.tensor_copy(self.f_dv, self.f_dv), "ve")
        return self.cnt("ve")

    def fence_sc(self):
        self.inc(self.nc.scalar.activation(self.f_sc, self.f_sc, AF.Identity),
                 "se")
        return self.cnt("se")

    def fence_po(self):
        self.inc(self.nc.gpsimd.tensor_copy(self.f_po, self.f_po), "po")
        return self.cnt("po")

    def stall(self, eng_kind):
        if eng_kind == "dv":
            self.nc.vector.tensor_copy(self.f_dv[:, 0:8], self.f_dv[:, 0:8])
        elif eng_kind == "sc":
            self.nc.scalar.activation(self.f_sc[:, 0:8], self.f_sc[:, 0:8],
                                      AF.Identity)
        elif eng_kind == "po":
            self.nc.gpsimd.tensor_copy(self.f_po[:, 0:8], self.f_po[:, 0:8])


def t128(v):
    v = np.asarray(v, np.float32)
    return np.ascontiguousarray(v.reshape(-1, 128).T)


def emit_ln(p, te, sc, dv, x_tiles, x2_tiles, xn_tiles, ndt, T,
            ones_col, ones_row, rowbuf, RB, ps_s1, ps_s2, ps_M, ps_V,
            neg_half, lnv_t, rstd_t, ms_row=None):
    for k in range(ndt):
        p.inc(dv.tensor_mul(x2_tiles[k], x_tiles[k], x_tiles[k]), "ve")
    p.fence_dv()
    p.wait(te, "ve")
    for k in range(ndt):
        te.matmul(ps_s1[0:1, 0:T], ones_col, x_tiles[k],
                  start=(k == 0), stop=(k == ndt - 1))
    for k in range(ndt):
        ins = te.matmul(ps_s2[0:1, 0:T], ones_col, x2_tiles[k],
                        start=(k == 0), stop=(k == ndt - 1))
    p.inc(ins, "pe")
    if ms_row is None:
        m_r, v_r, ms_r = (rowbuf[:, i * RB:i * RB + T] for i in range(3))
    else:
        m_r, v_r = (rowbuf[:, i * RB:i * RB + T] for i in range(2))
        ms_r = ms_row
    p.wait(dv, "pe")
    p.stall("dv")
    dv.tensor_scalar_mul(m_r, ps_s1[0:1, 0:T], 1.0 / D)
    dv.tensor_scalar(v_r, ps_s2[0:1, 0:T], 1.0 / D, EPS, ALU.mult, ALU.add)
    p.stall("dv")
    dv.tensor_mul(ms_r, m_r, m_r)
    p.stall("dv")
    p.stall("dv")
    dv.tensor_sub(v_r, v_r, ms_r)
    p.fence_dv()
    p.fence_dv()
    p.wait(te, "ve")
    p.inc(te.matmul(ps_M[:, 0:T], ones_row, m_r, start=True, stop=True), "pe")
    p.inc(te.matmul(ps_V[:, 0:T], ones_row, v_r, start=True, stop=True), "pe")
    p.wait(sc, "pe")
    p.stall("sc")
    sc.activation(lnv_t[:, 0:T], ps_V[:, 0:T], AF.Ln)
    p.inc(sc.activation(rstd_t[:, 0:T], lnv_t[:, 0:T], AF.Exp,
                        scale=neg_half[:, 0:1]), "se")
    p.fence_sc()
    p.wait(dv, "se")
    p.stall("dv")
    for k in range(ndt):
        dv.tensor_tensor(x2_tiles[k], x_tiles[k], ps_M[:, 0:T], ALU.subtract)
    for k in range(ndt):
        ins = dv.tensor_mul(xn_tiles[k], x2_tiles[k], rstd_t[:, 0:T])
    p.inc(ins, "ve")
    p.fence_dv()


def build_L1(n_chunks=None):
    p = Prog()
    nc = p.nc
    x_fm = p.din("x_fm", [D, LH])
    xhalo = p.din("xhalo", [D, 2])
    hmask = p.din("hmask", [128, 1])
    W_in = p.din("W_in", [D, 2 * DI])
    b_in_t = p.din("b_in_t", [128, 32])
    convw_t = p.din("convw_t", [128, 48])
    conv_b_t = p.din("conv_b_t", [128, 16])
    W_x = p.din("W_x", [DI, 96])
    u_conv = p.dout("u_conv", [DI, LH])
    res_o = p.dout("res", [DI, LH])
    xp_o = p.dout("xp", [96, LH])

    TC = 512
    NCH = LH // TC
    DT_IN = D // 128
    DT_U = DI // 128
    S = TC + 2

    ones_col = p.sbuf("ones_col", [128, 1], F32)
    ones_row = p.sbuf("ones_row", [1, 128], F32)
    w_in_s = p.sbuf("w_in_s", [128, DT_IN * 2 * DI], BF16)
    w_x_s = p.sbuf("w_x_s", [128, DT_U * 96], BF16)
    b_in_s = p.sbuf("b_in_s", [128, 32], F32)
    convw_s = p.sbuf("convw_s", [128, 48], F32)
    convb_s = p.sbuf("convb_s", [128, 16], F32)
    hmask_s = p.sbuf("hmask_s", [128, 1], F32)
    x_s = p.sbuf("x_s", [128, DT_IN * S], F32)
    x2_s = p.sbuf("x2_s", [128, DT_IN * S], F32)
    xn_s = p.sbuf("xn_s", [128, DT_IN * S], BF16)
    rowbuf = p.sbuf("rowbuf", [1, 3 * S], F32)
    upre = p.sbuf("upre", [128, DT_U * S], F32)
    uconv_s = p.sbuf("uconv_s", [128, DT_U * TC], F32)
    uconv_b = p.sbuf("uconv_b", [128, DT_U * TC], BF16)
    res_s = p.sbuf("res_s", [128, 2 * TC], F32)
    xp_s = p.sbuf("xp_s", [128, TC], F32)
    neg_half = p.sbuf("neg_half", [128, 1], F32)
    rstd_t = p.sbuf("rstd_t", [128, TC], F32)
    lnv_t = x2_s

    ps_s1 = p.psum("ps_s1", [1, TC])
    ps_s2 = p.psum("ps_s2", [1, TC])
    ps_M = p.psum("ps_M", [128, TC])
    ps_V = p.psum("ps_V", [128, TC])
    ps_mm = [p.psum("ps_mm0", [128, TC]), p.psum("ps_mm1", [128, TC])]
    ps_xp = p.psum("ps_xp", [128, TC])

    for s in ["ld", "pe", "ve", "se", "st", "po"]:
        p.sem(s)
    p.init_fences()
    sy, gp, te, dv, sc = nc.sync, nc.gpsimd, nc.tensor, nc.vector, nc.scalar

    for k in range(DT_IN):
        p.inc(gp.dma_start(out=w_in_s[:, k * 2 * DI:(k + 1) * 2 * DI],
                           in_=W_in[k * 128:(k + 1) * 128, :]), "ld", 16)
    for k in range(DT_U):
        p.inc(gp.dma_start(out=w_x_s[:, k * 96:(k + 1) * 96],
                           in_=W_x[k * 128:(k + 1) * 128, :]), "ld", 16)
    p.inc(sy.dma_start(out=b_in_s, in_=b_in_t), "ld", 16)
    p.inc(sy.dma_start(out=convw_s, in_=convw_t), "ld", 16)
    p.inc(sy.dma_start(out=convb_s, in_=conv_b_t), "ld", 16)
    p.inc(sy.dma_start(out=hmask_s, in_=hmask), "ld", 16)
    LD_ALL = p.cnt("ld")
    p.wait(dv, "ld", LD_ALL)
    p.inc(dv.memset(ones_col, 1.0), "ve")
    p.inc(dv.memset(ones_row, 1.0), "ve")
    p.inc(dv.memset(neg_half, -0.5), "ve")
    p.fence_dv()

    g = dict(x_ve=0, x_pe=0, x_se=0,
             bank_se=[0, 0],
             store_st=0, res_st=[0, 0],
             upre_halo_ve=0)

    chunks = [(-2, 2, True)] + [(i * TC, TC, False) for i in range(NCH)]
    if n_chunks is not None:
        chunks = chunks[:n_chunks]
    for (t0, T, mini) in chunks:
        xin = xhalo if mini else x_fm[:, t0:t0 + T]
        p.wait(sy, "ve", g["x_ve"])
        p.wait(sy, "pe", g["x_pe"])
        p.wait(sy, "se", g["x_se"])
        xld0 = p.cnt("ld")
        for k in range(DT_IN):
            p.inc(sy.dma_start(out=x_s[:, k * S:k * S + T],
                               in_=xin[k * 128:(k + 1) * 128, :]), "ld", 16)
        XLD = p.cnt("ld")
        p.wait(sc, "ld", XLD)
        p.wait(te, "ld", XLD)
        p.wait(dv, "ld", XLD)

        x_t = [x_s[:, k * S:k * S + T] for k in range(DT_IN)]
        x2_t = [x2_s[:, k * S:k * S + T] for k in range(DT_IN)]
        xn_t = [xn_s[:, k * S:k * S + T] for k in range(DT_IN)]
        emit_ln(p, te, sc, dv, x_t, x2_t, xn_t, DT_IN, T,
                ones_col, ones_row, rowbuf, S, ps_s1, ps_s2, ps_M, ps_V,
                neg_half, lnv_t, rstd_t)
        XN = p.cnt("ve")
        g["x_ve"], g["x_pe"], g["x_se"] = XN, p.cnt("pe"), p.cnt("se")

        p.wait(te, "ve", XN)
        nt = DT_U if mini else 2 * DT_U
        if not mini:
            p.wait(sc, "st", g["store_st"])
            p.wait(dv, "st", g["store_st"])
        for ot in range(nt):
            bank = ps_mm[ot % 2]
            p.wait(te, "se", g["bank_se"][ot % 2])
            for k in range(DT_IN):
                ins = te.matmul(
                    bank[:, 0:T],
                    w_in_s[:, k * 2 * DI + ot * 128: k * 2 * DI + ot * 128 + 128],
                    xn_s[:, k * S:k * S + T],
                    start=(k == 0), stop=(k == DT_IN - 1))
            p.inc(ins, "pe")
            p.wait(sc, "pe", p.cnt("pe"))
            if ot < DT_U:
                p.wait(sc, "ve", g["upre_halo_ve"])
                dst = upre[:, ot * S + 2: ot * S + 2 + T]
                p.inc(sc.activation(dst, bank[:, 0:T], AF.Identity,
                                    bias=b_in_s[:, ot:ot + 1]), "se")
            else:
                rt = ot - DT_U
                slot = rt % 2
                p.wait(sc, "st", g["res_st"][slot])
                p.inc(sc.activation(res_s[:, slot * TC: slot * TC + T],
                                    bank[:, 0:T], AF.Identity,
                                    bias=b_in_s[:, ot:ot + 1]), "se")
                p.wait(sy, "se", p.cnt("se"))
                p.inc(sy.dma_start(out=res_o[rt * 128:(rt + 1) * 128, t0:t0 + T],
                                   in_=res_s[:, slot * TC: slot * TC + T]),
                      "st", 16)
                g["res_st"][slot] = p.cnt("st")
            g["bank_se"][ot % 2] = p.cnt("se")
        WIN_SE = p.cnt("se")

        if mini:
            p.wait(dv, "se", WIN_SE)
            p.stall("dv")
            for ot in range(DT_U):
                dv.tensor_scalar_mul(upre[:, ot * S + 2:ot * S + 4],
                                     upre[:, ot * S + 2:ot * S + 4],
                                     hmask_s[:, 0:1])
            for ot in range(DT_U):
                ins = dv.tensor_copy(upre[:, ot * S:ot * S + 2],
                                     upre[:, ot * S + 2:ot * S + 4])
            p.inc(ins, "ve")
            g["upre_halo_ve"] = p.cnt("ve")
            continue

        p.wait(dv, "se", WIN_SE)
        for ot in range(DT_U):
            ub = upre[:, ot * S:(ot + 1) * S]
            uc = uconv_s[:, ot * TC:ot * TC + T]
            dv.tensor_scalar_mul(uc, ub[:, 0:T], convw_s[:, 3 * ot:3 * ot + 1])
            dv.scalar_tensor_tensor(uc, ub[:, 1:1 + T],
                                    convw_s[:, 3 * ot + 1:3 * ot + 2],
                                    uc, ALU.mult, ALU.add)
            dv.scalar_tensor_tensor(uc, ub[:, 2:2 + T],
                                    convw_s[:, 3 * ot + 2:3 * ot + 3],
                                    uc, ALU.mult, ALU.add)
            ins = dv.tensor_copy(upre[:, ot * S:ot * S + 2],
                                 upre[:, ot * S + T:ot * S + T + 2])
        p.inc(ins, "ve")
        CONV = p.cnt("ve")
        g["upre_halo_ve"] = CONV
        p.wait(sc, "ve", CONV)
        for ot in range(DT_U):
            ins = sc.activation(uconv_s[:, ot * TC:ot * TC + T],
                                uconv_s[:, ot * TC:ot * TC + T], AF.Silu,
                                bias=convb_s[:, ot:ot + 1])
        p.inc(ins, "se")
        SIL = p.cnt("se")
        p.wait(dv, "se", SIL)
        for ot in range(DT_U):
            ins = dv.tensor_copy(uconv_b[:, ot * TC:ot * TC + T],
                                 uconv_s[:, ot * TC:ot * TC + T])
        p.inc(ins, "ve")
        UB = p.cnt("ve")

        p.wait(te, "ve", UB)
        for k in range(DT_U):
            ins = te.matmul(ps_xp[0:96, 0:T], w_x_s[:, k * 96:(k + 1) * 96],
                            uconv_b[:, k * TC:k * TC + T],
                            start=(k == 0), stop=(k == DT_U - 1))
        p.inc(ins, "pe")
        p.wait(dv, "pe", p.cnt("pe"))
        p.inc(dv.tensor_copy(xp_s[0:96, 0:T], ps_xp[0:96, 0:T]), "ve")
        XPE = p.cnt("ve")

        p.wait(sy, "se", SIL)
        p.wait(sy, "ve", XPE)
        for ot in range(DT_U):
            p.inc(sy.dma_start(out=u_conv[ot * 128:(ot + 1) * 128, t0:t0 + T],
                               in_=uconv_s[:, ot * TC:ot * TC + T]), "st", 16)
        p.inc(sy.dma_start(out=xp_o[:, t0:t0 + T], in_=xp_s[0:96, 0:T]), "st", 16)
        g["store_st"] = p.cnt("st")

    p.wait(sy, "st")
    p.close()
    return p


def prep_L1(inputs):
    x = np.asarray(inputs["L0_attn_x"], np.float32)
    W_in = np.ascontiguousarray(np.asarray(inputs["L0_attn_W_in"], np.float32))
    b_in_t = t128(inputs["L0_attn_b_in"])
    cw = np.asarray(inputs["L0_attn_conv_w"], np.float32)[:, 0, :]
    convw_t = np.ascontiguousarray(
        cw.reshape(16, 128, 3).transpose(1, 0, 2).reshape(128, 48))
    conv_b_t = t128(inputs["L0_attn_conv_b"])
    W_x = np.ascontiguousarray(np.asarray(inputs["L0_attn_W_x"], np.float32))
    in_maps = []
    for c in range(8):
        b, h = c // 2, c % 2
        xbT = np.ascontiguousarray(x[b].T)
        x_fm = np.ascontiguousarray(xbT[:, h * LH:(h + 1) * LH])
        if h == 1:
            xhalo = np.ascontiguousarray(xbT[:, LH - 2:LH])
            hm = np.ones((128, 1), np.float32)
        else:
            xhalo = np.zeros((D, 2), np.float32)
            hm = np.zeros((128, 1), np.float32)
        in_maps.append(dict(x_fm=x_fm, xhalo=xhalo, hmask=hm, W_in=W_in,
                            b_in_t=b_in_t, convw_t=convw_t, conv_b_t=conv_b_t,
                            W_x=W_x))
    return in_maps


def np_ln(x, axis=-1):
    m = x.mean(axis, keepdims=True)
    v = x.var(axis, keepdims=True)
    return (x - m) / np.sqrt(v + EPS)


def np_silu(x):
    return x / (1.0 + np.exp(-x))


def build_L2(n_chunks=None, dbg=False):
    p = Prog()
    nc = p.nc
    u_i = p.din("u_s", [DIH, L])
    res_i = p.din("res_s", [DIH, L])
    xp_dt = p.din("xp_dt", [DTR, L])
    xp_B = p.din("xp_B", [NST, L])
    xp_C = p.din("xp_C", [NST, L])
    A_log_s = p.din("A_log_s", [DIH, NST])
    W_dt_s = p.din("W_dt_s", [DTR, DIH])
    b_dt_t = p.din("b_dt_t", [128, 8])
    W_out_s = p.din("W_out_s", [DIH, D])
    esel = p.din("esel", [NST, NST * 128])
    pout = p.dout("pout", [D, L])
    if dbg:
        dbg_delta = p.dout("dbg_delta", [DIH, 512])
        dbg_brep = p.dout("dbg_brep", [128, 2 * 512])
        dbg_state = p.dout("dbg_state", [DIH, NST])
        dbg_ym = p.dout("dbg_ym", [DIH, 512])

    T = 512
    NCH = L // T
    if n_chunks is not None:
        NCH = n_chunks
    NDT = DIH // 128

    for sname in ["ld", "uld", "pe", "ve", "se", "po", "st"]:
        p.sem(sname)
    p.init_fences(pool=True)
    sy, gp, te, dv, sc, po = (nc.sync, nc.gpsimd, nc.tensor, nc.vector,
                              nc.scalar, nc.gpsimd)

    esel_s = p.sbuf("esel_s", [NST, NST * 128], BF16)
    wdt_s = p.sbuf("wdt_s", [DTR, DIH], BF16)
    bdt_s = p.sbuf("bdt_s", [128, 8], F32)
    A_s = p.sbuf("A_s", [128, NDT * NST], F32)
    state = p.sbuf("state", [128, NDT * NST], F32)
    u_c1 = p.sbuf("u_c", [128, NDT * T], BF16)
    res_c2 = [p.sbuf(f"res_c{i}", [128, NDT * T], BF16) for i in range(2)]
    delta_c2 = [p.sbuf(f"delta_c{i}", [128, NDT * T], BF16) for i in range(2)]
    brep2 = [p.sbuf(f"brep{i}", [128, NST * T], BF16) for i in range(2)]
    du_b2 = [p.sbuf(f"du_b{i}", [128, NDT * T], BF16) for i in range(2)]
    crep2 = [p.sbuf(f"crep{i}", [128, NST * T], BF16) for i in range(2)]
    xpb_c = p.sbuf("xpb_c", [NST, T], BF16)
    xpc_c = p.sbuf("xpc_c", [NST, T], BF16)
    dtb_c = p.sbuf("dtb_c", [DTR, T], BF16)
    H = [p.sbuf("H0", [128, NST * T], BF16), p.sbuf("H1", [128, NST * T], BF16)]
    DA_all = p.sbuf("DA_all", [128, NST * T], BF16)
    DBU = [p.sbuf(f"DBU{i}", [128, T], BF16) for i in range(2)]
    wout_r = p.sbuf("wout_r", [128, NDT * D], BF16)
    ymb = p.sbuf("ymb", [128, NDT * T], BF16)
    pout_s = [p.sbuf("pout_s0", [128, T], F32)]

    ps_B = [p.psum("ps_B0", [128, T]), p.psum("ps_B1", [128, T])]
    ps_C = [p.psum("ps_C0", [128, T]), p.psum("ps_C1", [128, T])]
    ps_dl = [p.psum("ps_dl0", [128, T]), p.psum("ps_dl1", [128, T])]
    ps_w = [p.psum("ps_w0", [128, T]), p.psum("ps_w1", [128, T])]

    p.inc(gp.dma_start(out=esel_s, in_=esel), "ld", 16)
    p.inc(gp.dma_start(out=wdt_s, in_=W_dt_s), "ld", 16)
    for k in range(NDT):
        p.inc(gp.dma_start(out=wout_r[:, k * D:(k + 1) * D],
                           in_=W_out_s[k * 128:(k + 1) * 128, :]), "ld", 16)
    p.inc(sy.dma_start(out=bdt_s, in_=b_dt_t), "ld", 16)
    for k in range(NDT):
        p.inc(sy.dma_start(out=A_s[:, k * NST:(k + 1) * NST],
                           in_=A_log_s[k * 128:(k + 1) * 128, :]), "ld", 16)
    LD0 = p.cnt("ld")
    p.wait(sc, "ld", LD0)
    p.inc(sc.activation(A_s, A_s, AF.Exp), "se")
    p.fence_sc()
    p.wait(dv, "se")
    p.stall("dv")
    p.inc(dv.tensor_scalar_mul(A_s, A_s, -1.0), "ve")
    p.inc(dv.memset(state, 0.0), "ve")
    p.fence_dv()
    p.fence_dv()

    g = dict(u_ve=0, res_po=0, brep_ve=0, crep_po=0, delta_ve=0, delta_se=0,
             u_ve2=[0, 0], res_ve2=[0, 0], brep_ve2=[0, 0], crep_po2=[0, 0],
             delta_ve2=[0, 0],
             psB_se=[0, 0], psC_se=[0, 0], psdl_se=[0, 0], psw_se=[0, 0],
             hpo=[0, 0],
             da_ve=[0, 0, 0], da_mid=0,
             wout_ld=[0, 0], pout_st=[0],
             xp_pe=0)

    for c in range(NCH):
        t0 = c * T
        pc = c % 2
        u_c, res_c, delta_c = u_c1, res_c2[pc], delta_c2[pc]
        brep, du_b, crep = brep2[pc], du_b2[pc], crep2[pc]
        p.wait(sy, "ve", g["u_ve"])
        p.wait(sy, "ve", g["res_ve2"][pc])
        p.wait(sy, "pe", g["xp_pe"])
        uld0 = p.cnt("uld")
        for k in range(NDT):
            p.inc(gp.dma_start(out=u_c[:, k * T:(k + 1) * T],
                               in_=u_i[k * 128:(k + 1) * 128, t0:t0 + T]),
                  "uld", 16)
            p.inc(gp.dma_start(out=res_c[:, k * T:(k + 1) * T],
                               in_=res_i[k * 128:(k + 1) * 128, t0:t0 + T]),
                  "uld", 16)
        p.inc(gp.dma_start(out=xpb_c, in_=xp_B[:, t0:t0 + T]), "uld", 16)
        p.inc(gp.dma_start(out=xpc_c, in_=xp_C[:, t0:t0 + T]), "uld", 16)
        p.wait(gp, "pe", g["xp_pe"])
        p.inc(gp.dma_start(out=dtb_c, in_=xp_dt[:, t0:t0 + T]), "uld", 16)
        ULD = p.cnt("uld")

        p.wait(te, "ld", LD0)
        p.wait(te, "uld", ULD)
        for n in range(NST):
            pb = n % 2
            p.wait(te, "se", g["psB_se"][pb])
            ins = te.matmul(ps_B[pb], esel_s[:, n * 128:(n + 1) * 128],
                            xpb_c, start=True, stop=True)
            p.inc(ins, "pe")
            p.wait(te, "se", g["psC_se"][pb])
            ins = te.matmul(ps_C[pb], esel_s[:, n * 128:(n + 1) * 128],
                            xpc_c, start=True, stop=True)
            p.inc(ins, "pe")
            p.wait(sc, "pe", p.cnt("pe"))
            if c > 1 and n == 0:
                p.wait(sc, "ve", g["brep_ve2"][pc])
                p.wait(sc, "po", g["crep_po2"][pc])
            p.inc(sc.activation(brep[:, n * T:(n + 1) * T], ps_B[pb],
                                AF.Identity), "se")
            g["psB_se"][pb] = p.cnt("se")
            p.inc(sc.activation(crep[:, n * T:(n + 1) * T], ps_C[pb],
                                AF.Identity), "se")
            g["psC_se"][pb] = p.cnt("se")
        BREP_SE = p.cnt("se")

        for k in range(NDT):
            pb = k % 2
            p.wait(te, "se", g["psdl_se"][pb])
            ins = te.matmul(ps_dl[pb], wdt_s[:, k * 128:(k + 1) * 128],
                            dtb_c, start=True, stop=True)
            p.inc(ins, "pe")
            p.wait(sc, "pe", p.cnt("pe"))
            if c > 1 and k == 0:
                p.wait(sc, "ve", g["delta_ve2"][pc])
            dsl = delta_c[:, k * T:(k + 1) * T]
            sc.activation(dsl, ps_dl[pb], AF.Exp, bias=bdt_s[:, k:k + 1])
            p.inc(sc.activation(dsl, dsl, AF.Ln, bias=1.0), "se")
            g["psdl_se"][pb] = p.cnt("se")
        g["delta_se"] = p.cnt("se")
        for k in range(NDT):
            p.inc(sc.activation(res_c[:, k * T:(k + 1) * T],
                                res_c[:, k * T:(k + 1) * T], AF.Silu), "se")
        p.fence_sc()
        SRF = p.cnt("se")
        p.wait(dv, "se", SRF)
        p.stall("dv")
        for k in range(NDT):
            ins = dv.tensor_mul(du_b[:, k * T:(k + 1) * T],
                                delta_c[:, k * T:(k + 1) * T],
                                u_c[:, k * T:(k + 1) * T])
        p.inc(ins, "ve")
        g["u_ve"] = p.cnt("ve")

        def emit_tree(kk, POC):
            hs2 = H[kk % 2]
            p.wait(dv, "po", POC)
            p.stall("dv")
            p.stall("dv")
            dv.tensor_add(hs2[:, 0:8 * T], hs2[:, 0:8 * T],
                          hs2[:, 8 * T:16 * T])
            dv.tensor_add(hs2[:, 0:4 * T], hs2[:, 0:4 * T],
                          hs2[:, 4 * T:8 * T])
            dv.tensor_add(hs2[:, 0:2 * T], hs2[:, 0:2 * T],
                          hs2[:, 2 * T:4 * T])
            dv.tensor_add(hs2[:, 0:T], hs2[:, 0:T], hs2[:, T:2 * T])
            ins2 = dv.tensor_mul(ymb[:, kk * T:(kk + 1) * T], hs2[:, 0:T],
                                 res_c[:, kk * T:(kk + 1) * T])
            p.inc(ins2, "ve")

        pend = None
        for k in range(NDT):
            hs = H[k % 2]
            p.wait(sc, "ve", g["da_mid"])
            for n in range(NST):
                ins = sc.activation(DA_all[:, n * T:(n + 1) * T],
                                    delta_c[:, k * T:(k + 1) * T], AF.Exp,
                                    scale=A_s[:, k * NST + n:k * NST + n + 1])
            p.inc(ins, "se")
            p.fence_sc()
            DA_SE = p.cnt("se")
            p.wait(dv, "se", DA_SE)
            p.stall("dv")
            dv.tensor_mul(DBU[0], du_b[:, k * T:(k + 1) * T], brep[:, 0:T])
            for n in range(NST):
                if n + 1 < NST:
                    dv.tensor_mul(DBU[(n + 1) % 2],
                                  du_b[:, k * T:(k + 1) * T],
                                  brep[:, (n + 1) * T:(n + 2) * T])
                ins = dv.tensor_tensor_scan(
                    hs[:, n * T:(n + 1) * T], DA_all[:, n * T:(n + 1) * T],
                    DBU[n % 2], state[:, k * NST + n:k * NST + n + 1],
                    ALU.mult, ALU.add)
                p.inc(ins, "ve")
                if n == 8:
                    g["da_mid"] = p.cnt("ve")
            p.stall("dv")
            p.stall("dv")
            hv = hs.rearrange("p (n t) -> p n t", n=NST)
            p.inc(dv.tensor_copy(state[:, k * NST:(k + 1) * NST],
                                 hv[:, :, T - 1]), "ve")
            p.fence_dv()
            DV_K = p.cnt("ve")
            p.wait(po, "se", SRF)
            p.wait(po, "ve", DV_K)
            ins = po.tensor_mul(hs, hs, crep)
            p.inc(ins, "po")
            p.fence_po()
            if pend is not None:
                emit_tree(*pend)
            pend = (k, p.cnt("po"))
        emit_tree(*pend)
        g["res_ve2"][pc] = p.cnt("ve")
        g["crep_po2"][pc] = p.cnt("po")
        g["delta_ve2"][pc] = p.cnt("ve")
        g["brep_ve2"][pc] = p.cnt("ve")
        p.fence_dv()
        PO_F = p.cnt("ve")
        if dbg and c == 0:
            p.wait(sy, "se", p.cnt("se"))
            p.wait(sy, "ve", p.cnt("ve"))
            p.wait(sy, "po", PO_F)
            for k in range(NDT):
                p.inc(sy.dma_start(out=dbg_delta[k * 128:(k + 1) * 128, :],
                                   in_=delta_c[:, k * T:(k + 1) * T]), "st", 16)
                p.inc(sy.dma_start(out=dbg_state[k * 128:(k + 1) * 128, :],
                                   in_=state[:, k * NST:(k + 1) * NST]), "st", 16)
                p.inc(sy.dma_start(out=dbg_ym[k * 128:(k + 1) * 128, :],
                                   in_=res_c[:, k * T:(k + 1) * T]), "st", 16)
            p.inc(sy.dma_start(out=dbg_brep[:, 0:512], in_=brep[:, 0:T]), "st", 16)
            p.inc(sy.dma_start(out=dbg_brep[:, 512:1024],
                               in_=brep[:, 5 * T:6 * T]), "st", 16)
            DBG_ST = p.cnt("st")
            p.wait(sc, "st", DBG_ST)
            p.wait(dv, "st", DBG_ST)
            p.wait(po, "st", DBG_ST)

        p.wait(te, "ve", PO_F)
        p.wait(te, "ld", LD0)
        for ot in range(8):
            pw = ot % 2
            pq = 0
            p.wait(te, "se", g["psw_se"][pw])
            for k in range(NDT):
                ins = p.inc(te.matmul(ps_w[pw],
                                      wout_r[:, k * D + ot * 128:
                                             k * D + ot * 128 + 128],
                                      ymb[:, k * T:(k + 1) * T],
                                      start=(k == 0), stop=(k == NDT - 1)),
                            "pe")
            PE_W = p.cnt("pe")
            p.wait(sc, "pe", PE_W)
            p.wait(sc, "st", g["pout_st"][pq])
            p.inc(sc.activation(pout_s[pq], ps_w[pw], AF.Identity), "se")
            g["psw_se"][pw] = p.cnt("se")
            p.wait(sy, "se", p.cnt("se"))
            p.inc(sy.dma_start(out=pout[ot * 128:(ot + 1) * 128, t0:t0 + T],
                               in_=pout_s[pq]), "st", 16)
            g["pout_st"][pq] = p.cnt("st")
        g["xp_pe"] = p.cnt("pe")

    p.wait(sy, "st")
    p.close()
    return p


def prep_L2(inputs, l1_results):
    A_log = np.asarray(inputs["L0_attn_A_log"], np.float32)
    W_dt = np.asarray(inputs["L0_attn_W_dt"], np.float32)
    b_dt = np.asarray(inputs["L0_attn_b_dt"], np.float32)
    W_out = np.asarray(inputs["L0_attn_W_out"], np.float32)
    esel = np.zeros((NST, NST * 128), np.float32)
    for n in range(NST):
        esel[n, n * 128:(n + 1) * 128] = 1.0
    in_maps = []
    for c in range(8):
        b, dh = c // 2, c % 2
        u_full = np.concatenate([l1_results[2 * b]["u_conv"],
                                 l1_results[2 * b + 1]["u_conv"]], axis=1)
        res_full = np.concatenate([l1_results[2 * b]["res"],
                                   l1_results[2 * b + 1]["res"]], axis=1)
        xp_full = np.concatenate([l1_results[2 * b]["xp"],
                                  l1_results[2 * b + 1]["xp"]], axis=1)
        ds = slice(dh * DIH, (dh + 1) * DIH)
        in_maps.append(dict(
            u_s=np.ascontiguousarray(u_full[ds]),
            res_s=np.ascontiguousarray(res_full[ds]),
            xp_dt=np.ascontiguousarray(xp_full[0:DTR]),
            xp_B=np.ascontiguousarray(xp_full[DTR:DTR + NST]),
            xp_C=np.ascontiguousarray(xp_full[DTR + NST:DTR + 2 * NST]),
            A_log_s=np.ascontiguousarray(A_log[ds]),
            W_dt_s=np.ascontiguousarray(W_dt[:, ds]),
            b_dt_t=t128(b_dt[ds]),
            W_out_s=np.ascontiguousarray(W_out[ds]),
            esel=esel,
        ))
    return in_maps


def tilize(w, nk, no):
    t = w.reshape(nk, 128, no, 128).transpose(2, 1, 0, 3).reshape(no, 128, -1)
    return np.ascontiguousarray(t.astype(np.float32))


def build_L3():
    p = Prog()
    nc = p.nc
    x_fm = p.din("x_fm", [D, LH])
    p0 = p.din("p0", [D, LH])
    p1 = p.din("p1", [D, LH])
    b_out_t = p.din("b_out_t", [128, 8])
    w1T = p.din("w1T", [32, 128, 8 * 128])
    w2T = p.din("w2T", [32, 128, 8 * 128])
    h1T = p.din("h1T", [32, 128, 32 * 128])
    h2T = p.din("h2T", [32, 128, 32 * 128])
    w3T = p.din("w3T", [8, 128, 32 * 128])
    wvT = p.din("wvT", [D, D])
    w1s = p.din("w1s", [128, 32])
    w2s = p.din("w2s", [128, 32])
    h1s = p.din("h1s", [128, 32])
    h2s = p.din("h2s", [128, 32])
    w3s = p.din("w3s", [128, 8])
    x2_o = p.dout("x2", [D, LH])
    xn2_o = p.dout("xn2", [D, LH], BF16)
    v_o = p.dout("v_tm", [LH, D], BF16)
    hh_d = nc.dram_tensor("hh_d", [DH, LH], BF16, kind="Internal").ap()
    g1_d = nc.dram_tensor("g1_d", [DH, LH], BF16, kind="Internal").ap()
    g2_d = nc.dram_tensor("g2_d", [DH, LH], BF16, kind="Internal").ap()
    x1_d = nc.dram_tensor("x1_d", [D, LH], F32, kind="Internal").ap()

    T = 512
    NTC = LH // T
    ND = D // 128

    for sname in ["ld", "wld", "pe", "ve", "se", "st"]:
        p.sem(sname)
    p.init_fences()
    sy, gp, te, dv, sc = nc.sync, nc.gpsimd, nc.tensor, nc.vector, nc.scalar

    ones_col = p.sbuf("ones_col", [128, 1], F32)
    ones_row = p.sbuf("ones_row", [1, 128], F32)
    neg_half = p.sbuf("neg_half", [128, 1], F32)
    ps_s1 = p.psum("ps_s1", [128, T])
    ps_s2 = p.psum("ps_s2", [1, T])
    ps_M = p.psum("ps_M", [128, T])
    ps_V = p.psum("ps_V", [128, T])
    ps_a = [p.psum("ps_a0", [128, T]), p.psum("ps_a1", [128, T])]
    ps_b = [p.psum("ps_b0", [128, T]), p.psum("ps_b1", [128, T])]
    rowbuf = p.sbuf("rowbuf", [1, 2 * T], F32)
    rstd_t = p.sbuf("rstd_t", [128, T], F32)
    lnv_t = ps_s1
    scal = p.sbuf("scal", [128, 4 * 32 + 8 + 8], F32)
    RB = p.sbuf("RB", [128, 65536], BF16)
    SCR = p.sbuf("SCR", [128, 8192], F32)
    ab_t = p.sbuf("ab_t", [128, 4 * T], F32)
    outb = p.sbuf("outb", [128, 4 * T], BF16)
    xn2_c = p.sbuf("xn2_c", [128, 8 * T], BF16)
    wbuf = [p.sbuf(f"wbuf{i}", [128, 32 * 128], BF16) for i in range(2)]
    wbufB = [p.sbuf(f"wbufB{i}", [128, 8 * 128], BF16) for i in range(2)]


    OW1, OW2, OH1, OH2, OW3, OBO = 0, 32, 64, 96, 128, 136

    p.inc(sy.dma_start(out=scal[:, OW1:OW1 + 32], in_=w1s), "ld", 16)
    p.inc(sy.dma_start(out=scal[:, OW2:OW2 + 32], in_=w2s), "ld", 16)
    p.inc(sy.dma_start(out=scal[:, OH1:OH1 + 32], in_=h1s), "ld", 16)
    p.inc(sy.dma_start(out=scal[:, OH2:OH2 + 32], in_=h2s), "ld", 16)
    p.inc(sy.dma_start(out=scal[:, OW3:OW3 + 8], in_=w3s), "ld", 16)
    p.inc(sy.dma_start(out=scal[:, OBO:OBO + 8], in_=b_out_t), "ld", 16)
    LD0 = p.cnt("ld")
    p.wait(dv, "ld", LD0)
    p.inc(dv.memset(ones_col, 1.0), "ve")
    p.inc(dv.memset(ones_row, 1.0), "ve")
    p.inc(dv.memset(neg_half, -0.5), "ve")
    p.fence_dv()

    g = dict(psa_se=[0, 0], psb_se=[0, 0], ab_ve=[0, 0], out_st=[0, 0, 0, 0],
             w_pe=[0, 0], w2_pe=[0, 0], scr_ve=0, scr_pe=0, rb_pe=0,
             xn2_st=0)

    xv = SCR[:, 0:8 * T]
    pv = SCR[:, 8 * T:16 * T]
    qv = xv

    def ln_chunk(x_tiles, x2_tiles, xn_tiles):
        emit_ln(p, te, sc, dv, x_tiles, x2_tiles, xn_tiles, ND, T,
                ones_col, ones_row, rowbuf, T, ps_s1, ps_s2, ps_M, ps_V,
                neg_half, lnv_t, rstd_t, ms_row=rstd_t[0:1, 0:T])

    for tc in range(NTC):
        t0 = tc * T
        p.wait(sy, "ve", g["scr_ve"])
        p.wait(sy, "pe", g["scr_pe"])
        for k in range(ND):
            p.inc(sy.dma_start(out=pv[:, k * T:(k + 1) * T],
                               in_=p0[k * 128:(k + 1) * 128, t0:t0 + T]),
                  "ld", 16)
            p.inc(sy.dma_start(out=xv[:, k * T:(k + 1) * T],
                               in_=p1[k * 128:(k + 1) * 128, t0:t0 + T]),
                  "ld", 16)
        LDC = p.cnt("ld")
        p.wait(dv, "ld", LDC)
        for k in range(ND):
            sl = slice(k * T, (k + 1) * T)
            ins = dv.tensor_add(pv[:, sl], pv[:, sl], xv[:, sl])
        p.inc(ins, "ve")
        p.fence_dv()
        p.wait(sy, "ve", p.cnt("ve"))
        for k in range(ND):
            p.inc(sy.dma_start(out=xv[:, k * T:(k + 1) * T],
                               in_=x_fm[k * 128:(k + 1) * 128, t0:t0 + T]),
                  "ld", 16)
        LDC = p.cnt("ld")
        p.wait(dv, "ld", LDC)
        for k in range(ND):
            sl = slice(k * T, (k + 1) * T)
            ins = dv.tensor_add(pv[:, sl], pv[:, sl], xv[:, sl])
        p.inc(ins, "ve")
        p.fence_dv()
        p.wait(sc, "ve", p.cnt("ve"))
        for k in range(ND):
            sl = slice(k * T, (k + 1) * T)
            ins = sc.activation(pv[:, sl], pv[:, sl], AF.Identity,
                                bias=scal[:, OBO + k:OBO + k + 1])
        p.inc(ins, "se")
        p.fence_sc()
        SE_X1 = p.cnt("se")
        p.wait(sy, "se", SE_X1)
        for k in range(ND):
            p.inc(sy.dma_start(out=x1_d[k * 128:(k + 1) * 128, t0:t0 + T],
                               in_=pv[:, k * T:(k + 1) * T]), "st", 16)
        p.wait(dv, "se", SE_X1)
        p.stall("dv")
        x_t = [pv[:, k * T:(k + 1) * T] for k in range(ND)]
        x2_t = [qv[:, k * T:(k + 1) * T] for k in range(ND)]
        xn_t = [RB[:, k * 2048 + t0:k * 2048 + t0 + T] for k in range(ND)]
        ln_chunk(x_t, x2_t, xn_t)
        g["scr_ve"] = p.cnt("ve")
        g["scr_pe"] = p.cnt("pe")
    p.fence_dv()
    XN1_VE = p.cnt("ve")

    def stream_w(slab_ap, ot, buf, nk, which):
        p.wait(gp, "pe", g[which][ot % 2])
        p.inc(gp.dma_start(out=buf[:, 0:nk * 128], in_=slab_ap[ot]),
              "wld", 16)
        return p.cnt("wld")

    for ot in range(32):
        wa = wbuf[ot % 2]
        wb2 = wbufB[ot % 2]
        W1LD = stream_w(w1T, ot, wa, 8, "w_pe")
        W2LD = stream_w(w2T, ot, wb2, 8, "w2_pe")
        for tc in range(NTC):
            t0 = tc * T
            pa, pb = ps_a[tc % 2], ps_b[tc % 2]
            p.wait(te, "wld", W1LD)
            p.wait(te, "wld", W2LD)
            if ot == 0 and tc == 0:
                p.wait(te, "ve", XN1_VE)
            p.wait(te, "se", g["psa_se"][tc % 2])
            for k in range(ND):
                ins = te.matmul(pa, wa[:, k * 128:(k + 1) * 128],
                                RB[:, k * 2048 + t0:k * 2048 + t0 + T],
                                start=(k == 0), stop=(k == ND - 1))
            p.inc(ins, "pe")
            g["w_pe"][ot % 2] = p.cnt("pe")
            p.wait(te, "se", g["psb_se"][tc % 2])
            for k in range(ND):
                ins = te.matmul(pb, wb2[:, k * 128:(k + 1) * 128],
                                RB[:, k * 2048 + t0:k * 2048 + t0 + T],
                                start=(k == 0), stop=(k == ND - 1))
            p.inc(ins, "pe")
            g["w2_pe"][ot % 2] = p.cnt("pe")
            PEC = p.cnt("pe")
            p.wait(sc, "pe", PEC)
            p.wait(sc, "ve", g["ab_ve"][tc % 2])
            aa = ab_t[:, (tc % 2) * 2 * T:(tc % 2) * 2 * T + T]
            bb = ab_t[:, (tc % 2) * 2 * T + T:(tc % 2) * 2 * T + 2 * T]
            sc.activation(aa, pa, AF.Silu, scale=scal[:, OW1 + ot:OW1 + ot + 1])
            p.inc(sc.activation(bb, pb, AF.Identity,
                                scale=scal[:, OW2 + ot:OW2 + ot + 1]), "se")
            g["psa_se"][tc % 2] = p.cnt("se")
            g["psb_se"][tc % 2] = p.cnt("se")
            p.fence_sc()
            p.wait(dv, "se", p.cnt("se"))
            ob = outb[:, (tc % 2) * T:(tc % 2) * T + T]
            p.wait(dv, "st", g["out_st"][tc % 2])
            p.inc(dv.tensor_mul(ob, aa, bb), "ve")
            g["ab_ve"][tc % 2] = p.cnt("ve")
            p.fence_dv()
            p.wait(sy, "ve", p.cnt("ve"))
            p.inc(sy.dma_start(out=hh_d[ot * 128:(ot + 1) * 128, t0:t0 + T],
                               in_=ob), "st", 16)
            g["out_st"][tc % 2] = p.cnt("st")

    def big_stage(wT_ap, src_d, dst_d, scal_off, func, nk, no, dst_x1=False):
        p.wait(sy, "pe", p.cnt("pe"))
        p.wait(sy, "ve", p.cnt("ve"))
        p.wait(sy, "st", p.cnt("st"))
        for k in range(nk):
            p.inc(sy.dma_start(out=RB[:, k * 2048:(k + 1) * 2048],
                               in_=src_d[k * 128:(k + 1) * 128, :]), "ld", 16)
        RLD = p.cnt("ld")
        p.wait(te, "ld", RLD)
        for ot in range(no):
            wa = wbuf[ot % 2]
            WLD = stream_w(wT_ap, ot, wa, nk, "w_pe")
            for tc in range(NTC):
                t0 = tc * T
                pa = ps_a[tc % 2]
                p.wait(te, "wld", WLD)
                p.wait(te, "se", g["psa_se"][tc % 2])
                for k in range(nk):
                    ins = te.matmul(pa, wa[:, k * 128:(k + 1) * 128],
                                    RB[:, k * 2048 + t0:k * 2048 + t0 + T],
                                    start=(k == 0), stop=(k == nk - 1))
                p.inc(ins, "pe")
                g["w_pe"][ot % 2] = p.cnt("pe")
                PEC = p.cnt("pe")
                p.wait(sc, "pe", PEC)
                ob = outb[:, (2 + tc % 2) * T:(2 + tc % 2) * T + T]
                if dst_x1:
                    aa = ab_t[:, (tc % 2) * 2 * T:(tc % 2) * 2 * T + T]
                    p.wait(sc, "ve", g["ab_ve"][tc % 2])
                    p.inc(sc.activation(aa, pa, AF.Identity,
                                        scale=scal[:, scal_off + ot:
                                                   scal_off + ot + 1]), "se")
                    g["psa_se"][tc % 2] = p.cnt("se")
                else:
                    p.wait(sc, "st", g["out_st"][2 + tc % 2])
                    p.inc(sc.activation(ob, pa, func,
                                        scale=scal[:, scal_off + ot:
                                                   scal_off + ot + 1]), "se")
                    g["psa_se"][tc % 2] = p.cnt("se")
                    p.fence_sc()
                    p.wait(sy, "se", p.cnt("se"))
                    p.inc(sy.dma_start(
                        out=dst_d[ot * 128:(ot + 1) * 128, t0:t0 + T],
                        in_=ob), "st", 16)
                    g["out_st"][2 + tc % 2] = p.cnt("st")
                if dst_x1:
                    yield (ot, tc)

    for _ in big_stage(h1T, hh_d, g1_d, OH1, AF.Gelu, 32, 32):
        pass
    for _ in big_stage(h2T, g1_d, g2_d, OH2, AF.Gelu, 32, 32):
        pass

    p.wait(sy, "pe", p.cnt("pe"))
    p.wait(sy, "ve", p.cnt("ve"))
    p.wait(sy, "st", p.cnt("st"))
    for k in range(32):
        p.inc(sy.dma_start(out=RB[:, k * 2048:(k + 1) * 2048],
                           in_=g2_d[k * 128:(k + 1) * 128, :]), "ld", 16)
    RLD = p.cnt("ld")
    p.wait(te, "ld", RLD)
    wv_ld = 0
    for tc in range(NTC):
        t0 = tc * T
        p.wait(sy, "ve", g["scr_ve"])
        p.wait(sy, "pe", g["scr_pe"])
        for k in range(ND):
            p.inc(sy.dma_start(out=xv[:, k * T:(k + 1) * T],
                               in_=x1_d[k * 128:(k + 1) * 128, t0:t0 + T]),
                  "ld", 16)
        X1LD = p.cnt("ld")
        for ot in range(ND):
            wa = wbuf[ot % 2]
            WLD = stream_w(w3T, ot, wa, 32, "w_pe")
            pa = ps_a[ot % 2]
            p.wait(te, "wld", WLD)
            p.wait(te, "se", g["psa_se"][ot % 2])
            for k in range(32):
                ins = te.matmul(pa, wa[:, k * 128:(k + 1) * 128],
                                RB[:, k * 2048 + t0:k * 2048 + t0 + T],
                                start=(k == 0), stop=(k == 31))
            p.inc(ins, "pe")
            g["w_pe"][ot % 2] = p.cnt("pe")
            PEC = p.cnt("pe")
            p.wait(sc, "pe", PEC)
            aa = ab_t[:, (ot % 2) * 2 * T:(ot % 2) * 2 * T + T]
            p.wait(sc, "ve", g["ab_ve"][ot % 2])
            p.inc(sc.activation(aa, pa, AF.Identity,
                                scale=scal[:, OW3 + ot:OW3 + ot + 1]), "se")
            g["psa_se"][ot % 2] = p.cnt("se")
            p.fence_sc()
            p.wait(dv, "se", p.cnt("se"))
            p.wait(dv, "ld", X1LD)
            ins = dv.tensor_add(pv[:, ot * T:(ot + 1) * T], aa,
                                xv[:, ot * T:(ot + 1) * T])
            p.inc(ins, "ve")
            g["ab_ve"][ot % 2] = p.cnt("ve")
        p.fence_dv()
        X2VE = p.cnt("ve")
        p.wait(sy, "ve", X2VE)
        for k in range(ND):
            p.inc(sy.dma_start(out=x2_o[k * 128:(k + 1) * 128, t0:t0 + T],
                               in_=pv[:, k * T:(k + 1) * T]), "st", 16)
        x_t = [pv[:, k * T:(k + 1) * T] for k in range(ND)]
        x2_t = [qv[:, k * T:(k + 1) * T] for k in range(ND)]
        xn_t = [xn2_c[:, k * T:(k + 1) * T] for k in range(ND)]
        p.wait(dv, "st", g["xn2_st"])
        p.wait(dv, "pe", g["scr_pe"])
        ln_chunk(x_t, x2_t, xn_t)
        XN2VE = p.cnt("ve")
        p.wait(sy, "ve", XN2VE)
        for k in range(ND):
            p.inc(sy.dma_start(out=xn2_o[k * 128:(k + 1) * 128, t0:t0 + T],
                               in_=xn2_c[:, k * T:(k + 1) * T]), "st", 16)
        p.wait(gp, "pe", g["w_pe"][0])
        p.wait(gp, "pe", g["w_pe"][1])
        for eh in range(2):
            for k in range(ND):
                p.inc(gp.dma_start(
                    out=wbuf[eh][:, k * 512:(k + 1) * 512],
                    in_=wvT[k * 128:(k + 1) * 128,
                            eh * 512:(eh + 1) * 512]), "wld", 16)
        wv_ld = p.cnt("wld")
        for sub in range(T // 128):
            s0 = t0 + sub * 128
            for eh in range(2):
                pa = ps_b[eh]
                p.wait(te, "wld", wv_ld)
                p.wait(te, "ve", XN2VE)
                p.wait(te, "se", g["psb_se"][eh])
                for k in range(ND):
                    ins = te.matmul(
                        pa, xn2_c[:, k * T + sub * 128:k * T + sub * 128 + 128],
                        wbuf[eh][:, k * 512:(k + 1) * 512],
                        start=(k == 0), stop=(k == ND - 1))
                p.inc(ins, "pe")
                PEC = p.cnt("pe")
                p.wait(sc, "pe", PEC)
                ob = outb[:, (2 + eh) * T:(3 + eh) * T]
                p.wait(sc, "st", g["out_st"][2 + eh])
                p.inc(sc.activation(ob, pa, AF.Identity), "se")
                g["psb_se"][eh] = p.cnt("se")
                p.fence_sc()
                p.wait(sy, "se", p.cnt("se"))
                p.inc(sy.dma_start(out=v_o[s0:s0 + 128,
                                           eh * 512:(eh + 1) * 512],
                                   in_=ob), "st", 16)
                g["out_st"][2 + eh] = p.cnt("st")
        g["xn2_st"] = p.cnt("st")
        g["scr_ve"] = p.cnt("ve")
        g["scr_pe"] = p.cnt("pe")
        g["w_pe"][0] = p.cnt("pe")
        g["w_pe"][1] = p.cnt("pe")

    p.wait(sy, "st")
    p.close()
    return p


def prep_L3(inputs, l2_results):
    x = np.asarray(inputs["L0_attn_x"], np.float32)
    b_out_t = t128(inputs["L0_attn_b_out"])
    w1T = tilize(np.asarray(inputs["L0_ffn_w1_w"], np.float32).T, 8, 32)
    w2T = tilize(np.asarray(inputs["L0_ffn_w2_w"], np.float32).T, 8, 32)
    h1T = tilize(np.asarray(inputs["L0_ffn_h1_w"], np.float32).T, 32, 32)
    h2T = tilize(np.asarray(inputs["L0_ffn_h2_w"], np.float32).T, 32, 32)
    w3T = tilize(np.asarray(inputs["L0_ffn_w3_w"], np.float32).T, 32, 8)
    wvT = np.ascontiguousarray(np.asarray(inputs["L1_attn_W_v"], np.float32).T)
    w1s = t128(inputs["L0_ffn_w1_s"])
    w2s = t128(inputs["L0_ffn_w2_s"])
    h1s = t128(inputs["L0_ffn_h1_s"])
    h2s = t128(inputs["L0_ffn_h2_s"])
    w3s = t128(inputs["L0_ffn_w3_s"])
    in_maps = []
    for c in range(8):
        b, h = c // 2, c % 2
        sl = slice(h * LH, (h + 1) * LH)
        in_maps.append(dict(
            x_fm=np.ascontiguousarray(x[b].T[:, sl]),
            p0=np.ascontiguousarray(l2_results[2 * b]["pout"][:, sl]),
            p1=np.ascontiguousarray(l2_results[2 * b + 1]["pout"][:, sl]),
            b_out_t=b_out_t, w1T=w1T, w2T=w2T, h1T=h1T, h2T=h2T, w3T=w3T,
            wvT=wvT, w1s=w1s, w2s=w2s, h1s=h1s, h2s=h2s, w3s=w3s,
        ))
    return in_maps


def build_L4(dbg=False):
    p = Prog()
    nc = p.nc
    xn_f = p.din("xn_f", [D, L], BF16)
    v_f = p.din("v_f", [L, D], BF16)
    wqk = p.din("wqk", [D, D])
    slope = p.din("slope", [128, 1])
    diff_c = p.din("diff_c", [128, 512])
    maskv = p.din("maskv", [128, 4 * 512])
    maskf = p.din("maskf", [128, 512])
    out_o = p.dout("attn_out", [16, D, 128])
    if dbg:
        dbg_q = p.dout("dbg_q", [128, 8 * 128], BF16)
        dbg_sco = p.dout("dbg_sco", [128, 512])
        dbg_rm = p.dout("dbg_rm", [128, 2])
        dbg_at = p.dout("dbg_at", [128, 512], BF16)
        dbg_z = p.dout("dbg_z", [128, D])

    ND = 8
    NKT = 32

    for sname in ["ld", "pe", "ve", "se", "st"]:
        p.sem(sname)
    p.init_fences()
    sy, gp, te, dv, sc = nc.sync, nc.gpsimd, nc.tensor, nc.vector, nc.scalar

    K_s = p.sbuf("K_s", [128, ND * L], BF16)
    V_s = p.sbuf("V_s", [128, NKT * D], BF16)
    wqk_s = p.sbuf("wqk_s", [128, ND * D], BF16)
    Q_s = p.sbuf("Q_s", [128, ND * 128], BF16)
    sco = p.sbuf("sco", [128, 8 * 512], F32)
    attn = p.sbuf("attn", [128, 8 * 512], BF16)
    attnT = p.sbuf("attnT", [128, NKT * 128], BF16)
    z_sb = p.sbuf("z_sb", [128, D], F32)
    outf = p.sbuf("outf", [128, ND * 128], F32)
    bias_t = p.sbuf("bias_t", [128, 512], F32)
    diff_s = p.sbuf("diff_s", [128, 512], F32)
    mask_s = p.sbuf("mask_s", [128, 4 * 512], F32)
    maskf_s = p.sbuf("maskf_s", [128, 512], F32)
    slope_s = p.sbuf("slope_s", [128, 1], F32)
    rowm = p.sbuf("rowm", [128, 2], F32)

    ps_q = p.psum("ps_q", [128, 128])
    ps_s = [p.psum("ps_s0", [128, 512]), p.psum("ps_s1", [128, 512])]
    ps_t = [p.psum("ps_t0", [128, 128], BF16), p.psum("ps_tb", [128, 128], BF16)]
    ps_z = [p.psum("ps_z0", [128, 512]), p.psum("ps_z1", [128, 512])]
    ps_t2 = p.psum("ps_t2", [128, 128])
    ident = p.sbuf("ident", [128, 128], BF16)
    identf = p.sbuf("identf", [128, 128], F32)
    ident_i = p.din("ident_i", [128, 128])

    for k in range(ND):
        p.inc(sy.dma_start(out=K_s[:, k * L:(k + 1) * L],
                           in_=xn_f[k * 128:(k + 1) * 128, :]), "ld", 16)
        p.inc(gp.dma_start(out=wqk_s[:, k * D:(k + 1) * D],
                           in_=wqk[k * 128:(k + 1) * 128, :]), "ld", 16)
    for k in range(NKT):
        p.inc(sy.dma_start(out=V_s[:, k * D:(k + 1) * D],
                           in_=v_f[k * 128:(k + 1) * 128, :]), "ld", 16)
    p.inc(sy.dma_start(out=diff_s, in_=diff_c), "ld", 16)
    p.inc(sy.dma_start(out=mask_s, in_=maskv), "ld", 16)
    p.inc(sy.dma_start(out=maskf_s, in_=maskf), "ld", 16)
    p.inc(sy.dma_start(out=slope_s, in_=slope), "ld", 16)
    p.inc(gp.dma_start(out=ident, in_=ident_i), "ld", 16)
    p.inc(sy.dma_start(out=identf, in_=ident_i), "ld", 16)
    LD0 = p.cnt("ld")
    p.wait(te, "ld", LD0)
    p.wait(dv, "ld", LD0)
    p.wait(sc, "ld", LD0)

    g = dict(q_se=0, s_se=[0, 0], sco_dv=[0, 0], attn_pe=0, at_se=0,
             z_se=[0, 0], zsb_pe=0, out_st=0, t2_se=0, pst_se=[0, 0])

    for qi in range(16):
        qg = 2 * qi + 1
        q0v = qg * 128
        klen = q0v + 128
        nkc = (klen + 511) // 512
        nkt = klen // 128

        for et in range(ND):
            p.wait(te, "se", g["q_se"])
            for k in range(ND):
                ins = te.matmul(ps_q,
                                wqk_s[:, k * D + et * 128:k * D + et * 128 + 128],
                                K_s[:, k * L + q0v:k * L + q0v + 128],
                                start=(k == 0), stop=(k == ND - 1))
            p.inc(ins, "pe")
            p.wait(sc, "pe", p.cnt("pe"))
            p.inc(sc.activation(Q_s[:, et * 128:(et + 1) * 128], ps_q,
                                AF.Identity), "se")
            g["q_se"] = p.cnt("se")
        p.fence_sc()
        QSE = p.cnt("se")

        p.wait(te, "se", QSE)
        for kc in range(nkc):
            pb = kc % 2
            k0 = kc * 512
            p.wait(te, "ve", g["sco_dv"][pb])
            for et in range(ND):
                ins = te.matmul(ps_s[pb], Q_s[:, et * 128:(et + 1) * 128],
                                K_s[:, et * L + k0:et * L + k0 + 512],
                                start=(et == 0), stop=(et == ND - 1))
            p.inc(ins, "pe")
            p.wait(dv, "pe", p.cnt("pe"))
            dv.tensor_scalar(bias_t, diff_s, float(k0 - q0v), slope_s[:, 0:1],
                             ALU.add, ALU.mult)
            sl = sco[:, kc * 512:(kc + 1) * 512]
            dv.scalar_tensor_tensor(sl, ps_s[pb], 1.0 / np.sqrt(D), bias_t,
                                    ALU.mult, ALU.add)
            ins = None
            if kc == nkc - 1:
                off = (q0v - k0) // 128
                ins = dv.tensor_add(sl, sl,
                                    mask_s[:, off * 512:(off + 1) * 512])
            if kc == 0:
                ins = dv.tensor_add(sl, sl, maskf_s)
            if ins is None:
                ins = dv.tensor_copy(rowm[:, 0:1], rowm[:, 0:1])
            p.inc(ins, "ve")
            g["sco_dv"][pb] = p.cnt("ve")
        p.inc(dv.tensor_reduce(rowm[:, 0:1], sco[:, 0:nkc * 512], AX.X,
                               ALU.max, negate=True), "ve")
        p.fence_dv()
        p.fence_dv()
        p.wait(sc, "ve", p.cnt("ve"))
        p.stall("sc")
        for kc in range(nkc):
            sl = sco[:, kc * 512:(kc + 1) * 512]
            ins = sc.activation(sl, sl, AF.Exp, bias=rowm[:, 0:1])
        p.inc(ins, "se")
        p.fence_sc()
        p.wait(dv, "se", p.cnt("se"))
        p.stall("dv")
        p.inc(dv.tensor_reduce(rowm[:, 1:2], sco[:, 0:nkc * 512], AX.X,
                               ALU.add), "ve")
        for _ in range(4):
            p.stall("dv")
        p.inc(dv.reciprocal(rowm[:, 0:1], rowm[:, 1:2]), "ve")
        for _ in range(4):
            p.stall("dv")
        p.wait(dv, "pe", g["attn_pe"])
        for kc in range(nkc):
            ins = dv.tensor_scalar_mul(attn[:, kc * 512:(kc + 1) * 512],
                                       sco[:, kc * 512:(kc + 1) * 512],
                                       rowm[:, 0:1])
        p.inc(ins, "ve")
        p.fence_dv()
        ATTN_VE = p.cnt("ve")

        p.wait(te, "ve", ATTN_VE)
        p.wait(te, "se", g["at_se"])
        for kt in range(nkt):
            p.wait(te, "se", g["pst_se"][kt % 2])
            ins = te.transpose(ps_t[kt % 2], attn[:, kt * 128:(kt + 1) * 128],
                               ident)
            p.inc(ins, "pe")
            p.wait(sc, "pe", p.cnt("pe"))
            p.inc(sc.activation(attnT[:, kt * 128:(kt + 1) * 128],
                                ps_t[kt % 2], AF.Identity), "se")
            g["pst_se"][kt % 2] = p.cnt("se")
        p.fence_sc()
        ATSE = p.cnt("se")
        g["at_se"] = ATSE

        p.wait(te, "se", ATSE)
        for eh in range(2):
            p.wait(te, "se", g["z_se"][eh])
            for kt in range(nkt):
                ins = te.matmul(ps_z[eh], attnT[:, kt * 128:(kt + 1) * 128],
                                V_s[:, kt * D + eh * 512:kt * D + eh * 512 + 512],
                                start=(kt == 0), stop=(kt == nkt - 1))
            p.inc(ins, "pe")
            p.wait(sc, "pe", p.cnt("pe"))
            p.wait(sc, "pe", g["zsb_pe"])
            p.inc(sc.activation(z_sb[:, eh * 512:(eh + 1) * 512], ps_z[eh],
                                AF.Identity), "se")
            g["z_se"][eh] = p.cnt("se")
        p.fence_sc()
        ZSE = p.cnt("se")

        p.wait(te, "se", ZSE)
        for et in range(ND):
            p.wait(te, "se", g["t2_se"])
            ins = te.transpose(ps_t2, z_sb[:, et * 128:(et + 1) * 128], identf)
            p.inc(ins, "pe")
            p.wait(sc, "pe", p.cnt("pe"))
            p.wait(sc, "st", g["out_st"])
            p.inc(sc.activation(outf[:, et * 128:(et + 1) * 128], ps_t2,
                                AF.Identity), "se")
            g["t2_se"] = p.cnt("se")
        g["zsb_pe"] = p.cnt("pe")
        p.fence_sc()
        p.wait(sy, "se", p.cnt("se"))
        for et in range(ND):
            p.inc(sy.dma_start(out=out_o[qi, et * 128:(et + 1) * 128, :],
                               in_=outf[:, et * 128:(et + 1) * 128]), "st", 16)
        g["out_st"] = p.cnt("st")
        if dbg and qi == 0:
            p.wait(sy, "ve", p.cnt("ve"))
            p.inc(sy.dma_start(out=dbg_q, in_=Q_s), "st", 16)
            p.inc(sy.dma_start(out=dbg_sco, in_=sco[:, 0:512]), "st", 16)
            p.inc(sy.dma_start(out=dbg_rm, in_=rowm), "st", 16)
            p.inc(sy.dma_start(out=dbg_at, in_=attn[:, 0:512]), "st", 16)
            p.inc(sy.dma_start(out=dbg_z, in_=z_sb), "st", 16)
            DB = p.cnt("st")
            p.wait(te, "st", DB)
            p.wait(dv, "st", DB)
            p.wait(sc, "st", DB)
    p.wait(sy, "st")
    p.close()
    return p


def prep_L4(inputs, l3_results):
    mk_D, L_ = D, L
    Wqk = np.ascontiguousarray(np.asarray(inputs["L1_attn_W_qk"], np.float32))
    sl = float(np.asarray(inputs["L1_attn_alibi_slope"], np.float32)[0])
    slope = np.full((128, 1), sl, np.float32)
    dq = np.arange(128, dtype=np.float32)[:, None]
    dk = np.arange(512, dtype=np.float32)[None, :]
    diff_c = np.ascontiguousarray(np.broadcast_to(dk - dq, (128, 512)))
    ident = np.eye(128, dtype=np.float32)
    in_maps = []
    for c in range(8):
        b, par = c // 2, c % 2
        xn = np.concatenate([np.asarray(l3_results[2 * b]["xn2"]),
                             np.asarray(l3_results[2 * b + 1]["xn2"])], axis=1)
        v = np.concatenate([np.asarray(l3_results[2 * b]["v_tm"]),
                            np.asarray(l3_results[2 * b + 1]["v_tm"])], axis=0)
        if par == 0:
            xn = np.concatenate([np.zeros((mk_D, 128), xn.dtype),
                                 xn[:, :L_ - 128]], axis=1)
            v = np.concatenate([np.zeros((128, mk_D), v.dtype),
                                v[:L_ - 128]], axis=0)
        maskv = np.zeros((128, 4 * 512), np.float32)
        for off in range(4):
            m = (dk <= off * 128 + dq)
            maskv[:, off * 512:(off + 1) * 512] = np.where(m, 0.0, -1e30)
        maskf = np.where(dk < (128 if par == 0 else 0) + 0 * dq, -1e30,
                         0.0).astype(np.float32)
        maskf = np.broadcast_to(maskf, (128, 512)).copy()
        in_maps.append(dict(xn_f=np.ascontiguousarray(xn),
                            v_f=np.ascontiguousarray(v), wqk=Wqk, slope=slope,
                            diff_c=diff_c, maskv=maskv, maskf=maskf,
                            ident_i=ident))
    return in_maps


def build_L5(n_experts=E, dbg=False):
    p = Prog()
    nc = p.nc
    x2_i = p.din("x2_i", [D, LH])
    at_i = p.din("at_i", [D, LH])
    w1T = p.din("w1T", [E, 32, 128, 8 * 128])
    w2T = p.din("w2T", [E, 8, 128, 32 * 128])
    wrt = p.din("wrt", [128, 8 * E])
    esel8 = p.din("esel8", [E, E * 128])
    id_i = p.din("id_i", [128, 128])
    out_o = p.dout("out_fm", [D, LH])
    if dbg:
        dbg_xn3 = p.dout("dbg_xn3", [D, 512], BF16)
        dbg_lg = p.dout("dbg_lg", [E, LH])
        dbg_ltm = p.dout("dbg_ltm", [128, 16 * E])
        dbg_gtm = p.dout("dbg_gtm", [128, 16 * E])
        dbg_G = p.dout("dbg_G", [128, LH])
        dbg_h = p.dout("dbg_h", [128, 512], BF16)

    T = 512
    NTC = LH // T
    ND = 8

    for sname in ["ld", "wld", "pe", "ve", "se", "st"]:
        p.sem(sname)
    p.init_fences()
    sy, gp, te, dv, sc = nc.sync, nc.gpsimd, nc.tensor, nc.vector, nc.scalar

    acc = p.sbuf("acc", [128, ND * LH], F32)
    xn3 = p.sbuf("xn3", [128, ND * LH], BF16)
    hbuf = p.sbuf("hbuf", [128, 16 * LH], BF16)
    tmp_c = p.sbuf("tmp_c", [128, ND * T], F32)
    G_sb = p.sbuf("G_sb", [128, LH], F32)
    wbuf = [p.sbuf(f"wbuf{i}", [128, 16 * 128], BF16) for i in range(2)]
    wrt_s = p.sbuf("wrt_s", [128, ND * E], BF16)
    esel_s = p.sbuf("esel_s", [E, E * 128], F32)
    ident_s = p.sbuf("ident_s", [128, 128], F32)
    grow_d = nc.dram_tensor("grow_d", [E, LH], F32, kind="Internal").ap()
    grow = p.sbuf("grow", [E, 128], F32)
    grow_sb = p.sbuf("grow_sb", [E, T], F32)
    lg = G_sb[0:8, :]
    s_sc = p.sbuf("s_sc", [128, 2 * T], F32)
    rowbuf = s_sc[0:1, 0:2 * T]
    ltm = s_sc[:, 0:128]
    gtm = s_sc[:, 128:256]
    rb4 = s_sc[:, 256:384]
    rb1 = s_sc[:, 384:400]
    rb2 = s_sc[:, 400:416]
    rb3 = s_sc[:, 416:432]
    ones_col = p.sbuf("ones_col", [128, 1], F32)
    ones_row = p.sbuf("ones_row", [1, 128], F32)
    neg_half = p.sbuf("neg_half", [128, 1], F32)

    ps_lg = p.psum("ps_lg", [E, T])
    ps_t8 = p.psum("ps_t8", [128, 128])
    ps_G = [p.psum("ps_G0", [128, T]), p.psum("ps_G1", [128, T])]
    ps_h = [p.psum("ps_h0", [128, T]), p.psum("ps_h1", [128, T])]
    ps_y = [p.psum("ps_y0", [128, T]), p.psum("ps_y1", [128, T])]
    lnv_t = ps_h[0]
    rstd_t = ps_h[1]

    p.inc(gp.dma_start(out=wrt_s, in_=wrt), "ld", 16)
    p.inc(sy.dma_start(out=esel_s, in_=esel8), "ld", 16)
    p.inc(sy.dma_start(out=ident_s, in_=id_i), "ld", 16)
    LD0 = p.cnt("ld")
    p.wait(dv, "ld", LD0)
    p.inc(dv.memset(ones_col, 1.0), "ve")
    p.inc(dv.memset(ones_row, 1.0), "ve")
    p.inc(dv.memset(neg_half, -0.5), "ve")
    p.fence_dv()

    g = dict(pslg_se=0, psg_se=[0, 0], psh_se=[0, 0], psy_dv=[0, 0],
             w_pe=[0, 0], hb_pe=0, acc_dv=0, t8_se=0)

    for tc in range(NTC):
        t0 = tc * T
        p.wait(sy, "ve", g.get("init_ve", 0))
        p.wait(sy, "pe", g.get("init_pe", 0))
        for k in range(ND):
            p.inc(sy.dma_start(out=acc[:, k * LH + t0:k * LH + t0 + T],
                               in_=x2_i[k * 128:(k + 1) * 128, t0:t0 + T]),
                  "ld", 16)
            p.inc(sy.dma_start(out=tmp_c[:, k * T:(k + 1) * T],
                               in_=at_i[k * 128:(k + 1) * 128, t0:t0 + T]),
                  "ld", 16)
        LDC = p.cnt("ld")
        p.wait(dv, "ld", LDC)
        p.stall("dv")
        for k in range(ND):
            ins = dv.tensor_add(acc[:, k * LH + t0:k * LH + t0 + T],
                                acc[:, k * LH + t0:k * LH + t0 + T],
                                tmp_c[:, k * T:(k + 1) * T])
        p.inc(ins, "ve")
        p.fence_dv()
        x_t = [acc[:, k * LH + t0:k * LH + t0 + T] for k in range(ND)]
        x2_t = [tmp_c[:, k * T:(k + 1) * T] for k in range(ND)]
        xn_t = [xn3[:, k * LH + t0:k * LH + t0 + T] for k in range(ND)]
        emit_ln(p, te, sc, dv, x_t, x2_t, xn_t, ND, T,
                ones_col, ones_row, rowbuf, T, ps_y[0], ps_y[1], ps_G[0],
                ps_G[1], neg_half, lnv_t, rstd_t,
                ms_row=G_sb[0:1, 0:T])
        g["init_ve"] = p.cnt("ve")
        g["init_pe"] = p.cnt("pe")
    p.fence_dv()
    XN3 = p.cnt("ve")

    p.wait(te, "ve", XN3)
    p.wait(te, "ld", LD0)
    for tc in range(NTC):
        t0 = tc * T
        p.wait(te, "se", g["pslg_se"])
        for k in range(ND):
            ins = te.matmul(ps_lg, wrt_s[:, k * E:(k + 1) * E],
                            xn3[:, k * LH + t0:k * LH + t0 + T],
                            start=(k == 0), stop=(k == ND - 1))
        p.inc(ins, "pe")
        p.wait(sc, "pe", p.cnt("pe"))
        p.inc(sc.activation(lg[:, t0:t0 + T], ps_lg, AF.Identity), "se")
        g["pslg_se"] = p.cnt("se")
    p.fence_sc()

    p.wait(te, "se", p.cnt("se"))
    for sub in range(16):
        p.wait(te, "se", g["t8_se"])
        ins = te.transpose(ps_t8[:, 0:8], lg[:, sub * 128:(sub + 1) * 128],
                           ident_s[0:8, 0:8])
        p.inc(ins, "pe")
        p.wait(sc, "pe", p.cnt("pe"))
        p.inc(sc.activation(ltm[:, sub * E:(sub + 1) * E], ps_t8[:, 0:8],
                            AF.Identity), "se")
        g["t8_se"] = p.cnt("se")
    p.fence_sc()
    p.wait(dv, "se", p.cnt("se"))
    p.stall("dv")

    for sub in range(16):
        dv.tensor_reduce(rb1[:, sub:sub + 1], ltm[:, sub * E:(sub + 1) * E],
                         AX.X, ALU.max, negate=True)
    for sub in range(16):
        dv.tensor_scalar(rb4[:, sub * E:(sub + 1) * E],
                         ltm[:, sub * E:(sub + 1) * E], -1.0, rb1[:, sub:sub + 1],
                         ALU.mult, ALU.is_equal)
    for sub in range(16):
        dv.scalar_tensor_tensor(rb4[:, sub * E:(sub + 1) * E],
                                rb4[:, sub * E:(sub + 1) * E], -1e30,
                                ltm[:, sub * E:(sub + 1) * E],
                                ALU.mult, ALU.add)
    for sub in range(16):
        ins = dv.tensor_reduce(rb2[:, sub:sub + 1],
                               rb4[:, sub * E:(sub + 1) * E], AX.X, ALU.max)
    p.inc(ins, "ve")
    p.fence_dv()
    p.wait(sc, "ve", p.cnt("ve"))
    p.stall("sc")
    for sub in range(16):
        sc.activation(gtm[:, sub * E:(sub + 1) * E],
                      ltm[:, sub * E:(sub + 1) * E], AF.Exp,
                      bias=rb1[:, sub:sub + 1])
    for sub in range(16):
        ins = sc.activation(rb3[:, sub:sub + 1], rb2[:, sub:sub + 1], AF.Exp,
                            bias=rb1[:, sub:sub + 1])
    p.inc(ins, "se")
    p.fence_sc()
    p.wait(dv, "se", p.cnt("se"))
    p.stall("dv")
    for sub in range(16):
        dv.tensor_scalar_add(rb3[:, sub:sub + 1], rb3[:, sub:sub + 1], 1.0)
    for sub in range(16):
        dv.reciprocal(rb3[:, sub:sub + 1], rb3[:, sub:sub + 1])
    for sub in range(16):
        dv.tensor_single_scalar(rb4[:, sub * E:(sub + 1) * E],
                                ltm[:, sub * E:(sub + 1) * E],
                                rb2[:, sub:sub + 1], ALU.is_ge)
    for sub in range(16):
        dv.tensor_mul(gtm[:, sub * E:(sub + 1) * E],
                      gtm[:, sub * E:(sub + 1) * E],
                      rb4[:, sub * E:(sub + 1) * E])
    for sub in range(16):
        ins = dv.tensor_scalar_mul(gtm[:, sub * E:(sub + 1) * E],
                                   gtm[:, sub * E:(sub + 1) * E],
                                   rb3[:, sub:sub + 1])
    p.inc(ins, "ve")
    p.fence_dv()
    p.fence_dv()

    p.wait(te, "ve", p.cnt("ve"))
    for sub in range(16):
        p.wait(te, "se", g["t8_se"])
        ins = te.transpose(ps_t8[0:8, :], gtm[:, sub * E:(sub + 1) * E],
                           ident_s)
        p.inc(ins, "pe")
        p.wait(sc, "pe", p.cnt("pe"))
        p.wait(sc, "st", g.get("grow_st", 0))
        p.inc(sc.activation(grow, ps_t8[0:8, 0:128], AF.Identity), "se")
        g["t8_se"] = p.cnt("se")
        p.fence_sc()
        p.wait(sy, "se", p.cnt("se"))
        p.inc(sy.dma_start(out=grow_d[:, sub * 128:(sub + 1) * 128],
                           in_=grow), "st", 16)
        g["grow_st"] = p.cnt("st")
    GROW_ST = p.cnt("st")

    if dbg:
        p.wait(sy, "se", p.cnt("se"))
        p.wait(sy, "ve", p.cnt("ve"))
        for k in range(ND):
            p.inc(sy.dma_start(out=dbg_xn3[k * 128:(k + 1) * 128, :],
                               in_=xn3[:, k * LH:k * LH + 512]), "st", 16)
        p.inc(sy.dma_start(out=dbg_lg, in_=lg), "st", 16)
        p.inc(sy.dma_start(out=dbg_ltm, in_=ltm.tensor.ap()[:, 0:128]), "st", 16)
        p.inc(sy.dma_start(out=dbg_gtm, in_=gtm.tensor.ap()[:, 128:256]), "st", 16)
        DBG1 = p.cnt("st")
        p.wait(sc, "st", DBG1)
        p.wait(dv, "st", DBG1)

    for e in range(n_experts):
        for tc in range(NTC):
            t0 = tc * T
            pb = tc % 2
            p.wait(sy, "st", GROW_ST)
            p.wait(sy, "pe", g.get("growsb_pe", 0))
            p.inc(sy.dma_start(out=grow_sb, in_=grow_d[:, t0:t0 + T]),
                  "ld", 16)
            p.wait(te, "ld", p.cnt("ld"))
            p.wait(te, "se", g["psg_se"][pb])
            ins = te.matmul(ps_G[pb], esel_s[:, e * 128:(e + 1) * 128],
                            grow_sb, start=True, stop=True)
            p.inc(ins, "pe")
            g["growsb_pe"] = p.cnt("pe")
            p.wait(sc, "pe", p.cnt("pe"))
            p.wait(sc, "ve", g["acc_dv"])
            p.inc(sc.activation(G_sb[:, t0:t0 + T], ps_G[pb], AF.Identity),
                  "se")
            g["psg_se"][pb] = p.cnt("se")
        p.fence_sc()
        GSE = p.cnt("se")

        for half in range(2):
            for oi in range(16):
                ot = half * 16 + oi
                wb = wbuf[oi % 2]
                p.wait(gp, "pe", g["w_pe"][oi % 2])
                p.inc(gp.dma_start(out=wb[:, 0:8 * 128],
                                   in_=w1T[e, ot]), "wld", 16)
                WLD = p.cnt("wld")
                p.wait(te, "wld", WLD)
                for tc in range(NTC):
                    t0 = tc * T
                    ph = ps_h[tc % 2]
                    p.wait(te, "se", g["psh_se"][tc % 2])
                    if e == 0 and half == 0 and oi == 0:
                        p.wait(te, "ve", XN3)
                    for k in range(ND):
                        ins = te.matmul(ph, wb[:, k * 128:(k + 1) * 128],
                                        xn3[:, k * LH + t0:k * LH + t0 + T],
                                        start=(k == 0), stop=(k == ND - 1))
                    p.inc(ins, "pe")
                    g["w_pe"][oi % 2] = p.cnt("pe")
                    p.wait(sc, "pe", p.cnt("pe"))
                    p.wait(sc, "pe", g["hb_pe"])
                    p.inc(sc.activation(hbuf[:, oi * LH + t0:oi * LH + t0 + T],
                                        ph, AF.Silu), "se")
                    g["psh_se"][tc % 2] = p.cnt("se")
            p.fence_sc()
            HSE = p.cnt("se")
            for dt in range(ND):
                wb = wbuf[dt % 2]
                p.wait(gp, "pe", g["w_pe"][dt % 2])
                p.inc(gp.dma_start(
                    out=wb[:, 0:16 * 128],
                    in_=w2T[e, dt, :, half * 16 * 128:(half + 1) * 16 * 128]),
                    "wld", 16)
                WLD = p.cnt("wld")
                p.wait(te, "wld", WLD)
                p.wait(te, "se", HSE)
                for tc in range(NTC):
                    t0 = tc * T
                    py = ps_y[tc % 2]
                    p.wait(te, "ve", g["psy_dv"][tc % 2])
                    for k in range(16):
                        ins = te.matmul(py, wb[:, k * 128:(k + 1) * 128],
                                        hbuf[:, k * LH + t0:k * LH + t0 + T],
                                        start=(k == 0), stop=(k == 15))
                    p.inc(ins, "pe")
                    g["w_pe"][dt % 2] = p.cnt("pe")
                    PEY = p.cnt("pe")
                    p.wait(dv, "pe", PEY)
                    p.wait(dv, "se", GSE)
                    dv.tensor_mul(tmp_c[:, dt * T:(dt + 1) * T], py,
                                  G_sb[:, t0:t0 + T])
                    ins = dv.tensor_add(acc[:, dt * LH + t0:dt * LH + t0 + T],
                                        acc[:, dt * LH + t0:dt * LH + t0 + T],
                                        tmp_c[:, dt * T:(dt + 1) * T])
                    p.inc(ins, "ve")
                    g["psy_dv"][tc % 2] = p.cnt("ve")
            g["hb_pe"] = p.cnt("pe")
            if dbg and e == 0 and half == 0:
                p.wait(sy, "se", p.cnt("se"))
                p.inc(sy.dma_start(out=dbg_h, in_=hbuf[:, 0:512]), "st", 16)
                p.inc(sy.dma_start(out=dbg_G, in_=G_sb), "st", 16)
                DBG2 = p.cnt("st")
                p.wait(sc, "st", DBG2)
                p.wait(te, "st", DBG2)
        g["acc_dv"] = p.cnt("ve")
    p.fence_dv()

    p.wait(sy, "ve", p.cnt("ve"))
    for k in range(ND):
        p.inc(sy.dma_start(out=out_o[k * 128:(k + 1) * 128, :],
                           in_=acc[:, k * LH:(k + 1) * LH]), "st", 16)
    p.wait(sy, "st")
    p.close()
    return p


def prep_L5(inputs, l3_results, l4_results):
    W1 = np.asarray(inputs["L1_ffn_W1"], np.float32)
    W2 = np.asarray(inputs["L1_ffn_W2"], np.float32)
    Wr = np.asarray(inputs["L1_ffn_Wr"], np.float32)
    w1T = np.stack([tilize(W1[e].T, 8, 32) for e in range(E)])
    w2T = np.stack([tilize(W2[e].T, 32, 8) for e in range(E)])
    wrt = np.ascontiguousarray(
        Wr.T.reshape(8, 128, E).transpose(1, 0, 2).reshape(128, 8 * E))
    esel8 = np.zeros((E, E * 128), np.float32)
    for e in range(E):
        esel8[e, e * 128:(e + 1) * 128] = 1.0
    at_full = []
    for b in range(B):
        a = np.zeros((D, L), np.float32)
        for par in range(2):
            got = np.asarray(l4_results[2 * b + par]["attn_out"])
            for qi in range(16):
                qg = 2 * qi + par
                a[:, qg * 128:(qg + 1) * 128] = got[qi]
        at_full.append(a)
    in_maps = []
    for c in range(8):
        b, h = c // 2, c % 2
        sl = slice(h * LH, (h + 1) * LH)
        in_maps.append(dict(
            x2_i=np.ascontiguousarray(np.asarray(l3_results[c]["x2"])),
            at_i=np.ascontiguousarray(at_full[b][:, sl]),
            w1T=w1T, w2T=w2T, wrt=wrt, esel8=esel8,
            id_i=np.eye(128, dtype=np.float32)))
    return in_maps




def kernel(**inputs):
    cores = list(range(8))
    p1 = build_L1()
    r1 = run_bass_kernel_spmd(p1.nc, prep_L1(inputs), core_ids=cores)
    p1.close if False else None
    p2 = build_L2()
    r2 = run_bass_kernel_spmd(p2.nc, prep_L2(inputs, r1.results),
                              core_ids=cores)
    p3 = build_L3()
    r3 = run_bass_kernel_spmd(p3.nc, prep_L3(inputs, r2.results),
                              core_ids=cores)
    p4 = build_L4()
    r4 = run_bass_kernel_spmd(p4.nc, prep_L4(inputs, r3.results),
                              core_ids=cores)
    p5 = build_L5()
    r5 = run_bass_kernel_spmd(p5.nc, prep_L5(inputs, r3.results, r4.results),
                              core_ids=cores)
    out = np.empty((B, L, D), np.float32)
    for c in range(8):
        b, h = c // 2, c % 2
        out[b, h * LH:(h + 1) * LH, :] = np.asarray(
            r5.results[c]["out_fm"]).T
    return out


# version 06
# speedup vs baseline: 1.0179x; 1.0179x over previous
import numpy as np
import ml_dtypes
from contextlib import ExitStack

import concourse.bass as bass
import concourse.mybir as mybir
from concourse.bass_utils import run_bass_kernel_spmd

F32 = mybir.dt.float32
BF16 = mybir.dt.bfloat16
AF = mybir.ActivationFunctionType
ALU = mybir.AluOpType
AX = mybir.AxisListType

B, L, D = 4, 4096, 1024
DI = 2048
NST = 16
DTR = 64
DH = 4096
E, DFF = 8, 4096
LH = L // 2
DIH = DI // 2
EPS = 1e-5


class Prog:

    def __init__(self):
        self.nc = bass.Bass("TRN2", target_bir_lowering=False)
        self.stack = ExitStack()
        self.sems = {}

    def sem(self, name):
        h = self.stack.enter_context(self.nc.semaphore(name))
        self.sems[name] = [h, 0]
        return h

    def inc(self, instr, name, amt=1):
        s = self.sems[name]
        instr.then_inc(s[0], amt)
        s[1] += amt
        return instr

    def cnt(self, name):
        return self.sems[name][1]

    def wait(self, eng, name, cnt=None):
        s = self.sems[name]
        eng.wait_ge(s[0], s[1] if cnt is None else cnt)

    def sbuf(self, name, shape, dt):
        return self.stack.enter_context(self.nc.sbuf_tensor(name, shape, dt)).ap()

    def psum(self, name, shape, dt=F32):
        return self.stack.enter_context(self.nc.psum_tensor(name, shape, dt)).ap()

    def din(self, name, shape, dt=F32):
        return self.nc.dram_tensor(name, shape, dt, kind="ExternalInput").ap()

    def dout(self, name, shape, dt=F32):
        return self.nc.dram_tensor(name, shape, dt, kind="ExternalOutput").ap()

    def close(self):
        self.stack.close()

    def init_fences(self, pool=False):
        self.f_dv = self.sbuf("fence_dv", [1, 128], F32)
        self.f_sc = self.f_dv
        self.f_po = self.f_dv

    def fence_dv(self):
        self.inc(self.nc.vector.tensor_copy(self.f_dv, self.f_dv), "ve")
        return self.cnt("ve")

    def fence_sc(self):
        self.inc(self.nc.scalar.activation(self.f_sc, self.f_sc, AF.Identity),
                 "se")
        return self.cnt("se")

    def fence_po(self):
        self.inc(self.nc.gpsimd.tensor_copy(self.f_po, self.f_po), "po")
        return self.cnt("po")

    def stall(self, eng_kind):
        if eng_kind == "dv":
            self.nc.vector.tensor_copy(self.f_dv[:, 0:8], self.f_dv[:, 0:8])
        elif eng_kind == "sc":
            self.nc.scalar.activation(self.f_sc[:, 0:8], self.f_sc[:, 0:8],
                                      AF.Identity)
        elif eng_kind == "po":
            self.nc.gpsimd.tensor_copy(self.f_po[:, 0:8], self.f_po[:, 0:8])


def t128(v):
    v = np.asarray(v, np.float32)
    return np.ascontiguousarray(v.reshape(-1, 128).T)


def emit_ln(p, te, sc, dv, x_tiles, x2_tiles, xn_tiles, ndt, T,
            ones_col, ones_row, rowbuf, RB, ps_s1, ps_s2, ps_M, ps_V,
            neg_half, lnv_t, rstd_t, ms_row=None):
    for k in range(ndt):
        p.inc(dv.tensor_mul(x2_tiles[k], x_tiles[k], x_tiles[k]), "ve")
    p.fence_dv()
    p.wait(te, "ve")
    for k in range(ndt):
        te.matmul(ps_s1[0:1, 0:T], ones_col, x_tiles[k],
                  start=(k == 0), stop=(k == ndt - 1))
    for k in range(ndt):
        ins = te.matmul(ps_s2[0:1, 0:T], ones_col, x2_tiles[k],
                        start=(k == 0), stop=(k == ndt - 1))
    p.inc(ins, "pe")
    if ms_row is None:
        m_r, v_r, ms_r = (rowbuf[:, i * RB:i * RB + T] for i in range(3))
    else:
        m_r, v_r = (rowbuf[:, i * RB:i * RB + T] for i in range(2))
        ms_r = ms_row
    p.wait(dv, "pe")
    p.stall("dv")
    dv.tensor_scalar_mul(m_r, ps_s1[0:1, 0:T], 1.0 / D)
    dv.tensor_scalar(v_r, ps_s2[0:1, 0:T], 1.0 / D, EPS, ALU.mult, ALU.add)
    p.stall("dv")
    dv.tensor_mul(ms_r, m_r, m_r)
    p.stall("dv")
    p.stall("dv")
    dv.tensor_sub(v_r, v_r, ms_r)
    p.fence_dv()
    p.fence_dv()
    p.wait(te, "ve")
    p.inc(te.matmul(ps_M[:, 0:T], ones_row, m_r, start=True, stop=True), "pe")
    p.inc(te.matmul(ps_V[:, 0:T], ones_row, v_r, start=True, stop=True), "pe")
    p.wait(sc, "pe")
    p.stall("sc")
    sc.activation(lnv_t[:, 0:T], ps_V[:, 0:T], AF.Ln)
    p.inc(sc.activation(rstd_t[:, 0:T], lnv_t[:, 0:T], AF.Exp,
                        scale=neg_half[:, 0:1]), "se")
    p.fence_sc()
    p.wait(dv, "se")
    p.stall("dv")
    for k in range(ndt):
        dv.tensor_tensor(x2_tiles[k], x_tiles[k], ps_M[:, 0:T], ALU.subtract)
    for k in range(ndt):
        ins = dv.tensor_mul(xn_tiles[k], x2_tiles[k], rstd_t[:, 0:T])
    p.inc(ins, "ve")
    p.fence_dv()


def build_L1(n_chunks=None):
    p = Prog()
    nc = p.nc
    x_fm = p.din("x_fm", [D, LH])
    xhalo = p.din("xhalo", [D, 2])
    hmask = p.din("hmask", [128, 1])
    W_in = p.din("W_in", [D, 2 * DI])
    b_in_t = p.din("b_in_t", [128, 32])
    convw_t = p.din("convw_t", [128, 48])
    conv_b_t = p.din("conv_b_t", [128, 16])
    W_x = p.din("W_x", [DI, 96])
    u_conv = p.dout("u_conv", [DI, LH])
    res_o = p.dout("res", [DI, LH])
    xp_o = p.dout("xp", [96, LH])

    TC = 512
    NCH = LH // TC
    DT_IN = D // 128
    DT_U = DI // 128
    S = TC + 2

    ones_col = p.sbuf("ones_col", [128, 1], F32)
    ones_row = p.sbuf("ones_row", [1, 128], F32)
    w_in_s = p.sbuf("w_in_s", [128, DT_IN * 2 * DI], BF16)
    w_x_s = p.sbuf("w_x_s", [128, DT_U * 96], BF16)
    b_in_s = p.sbuf("b_in_s", [128, 32], F32)
    convw_s = p.sbuf("convw_s", [128, 48], F32)
    convb_s = p.sbuf("convb_s", [128, 16], F32)
    hmask_s = p.sbuf("hmask_s", [128, 1], F32)
    x_s = p.sbuf("x_s", [128, DT_IN * S], F32)
    x2_s = p.sbuf("x2_s", [128, DT_IN * S], F32)
    xn_s = p.sbuf("xn_s", [128, DT_IN * S], BF16)
    rowbuf = p.sbuf("rowbuf", [1, 3 * S], F32)
    upre = p.sbuf("upre", [128, DT_U * S], F32)
    uconv_s = p.sbuf("uconv_s", [128, DT_U * TC], F32)
    uconv_b = p.sbuf("uconv_b", [128, DT_U * TC], BF16)
    res_s = p.sbuf("res_s", [128, 2 * TC], F32)
    xp_s = p.sbuf("xp_s", [128, TC], F32)
    neg_half = p.sbuf("neg_half", [128, 1], F32)
    rstd_t = p.sbuf("rstd_t", [128, TC], F32)
    lnv_t = x2_s

    ps_s1 = p.psum("ps_s1", [1, TC])
    ps_s2 = p.psum("ps_s2", [1, TC])
    ps_M = p.psum("ps_M", [128, TC])
    ps_V = p.psum("ps_V", [128, TC])
    ps_mm = [p.psum("ps_mm0", [128, TC]), p.psum("ps_mm1", [128, TC])]
    ps_xp = p.psum("ps_xp", [128, TC])

    for s in ["ld", "pe", "ve", "se", "st", "po"]:
        p.sem(s)
    p.init_fences()
    sy, gp, te, dv, sc = nc.sync, nc.gpsimd, nc.tensor, nc.vector, nc.scalar

    for k in range(DT_IN):
        p.inc(gp.dma_start(out=w_in_s[:, k * 2 * DI:(k + 1) * 2 * DI],
                           in_=W_in[k * 128:(k + 1) * 128, :]), "ld", 16)
    for k in range(DT_U):
        p.inc(gp.dma_start(out=w_x_s[:, k * 96:(k + 1) * 96],
                           in_=W_x[k * 128:(k + 1) * 128, :]), "ld", 16)
    p.inc(sy.dma_start(out=b_in_s, in_=b_in_t), "ld", 16)
    p.inc(sy.dma_start(out=convw_s, in_=convw_t), "ld", 16)
    p.inc(sy.dma_start(out=convb_s, in_=conv_b_t), "ld", 16)
    p.inc(sy.dma_start(out=hmask_s, in_=hmask), "ld", 16)
    LD_ALL = p.cnt("ld")
    p.wait(dv, "ld", LD_ALL)
    p.inc(dv.memset(ones_col, 1.0), "ve")
    p.inc(dv.memset(ones_row, 1.0), "ve")
    p.inc(dv.memset(neg_half, -0.5), "ve")
    p.fence_dv()

    g = dict(x_ve=0, x_pe=0, x_se=0,
             bank_se=[0, 0],
             store_st=0, res_st=[0, 0],
             upre_halo_ve=0)

    chunks = [(-2, 2, True)] + [(i * TC, TC, False) for i in range(NCH)]
    if n_chunks is not None:
        chunks = chunks[:n_chunks]
    for (t0, T, mini) in chunks:
        xin = xhalo if mini else x_fm[:, t0:t0 + T]
        p.wait(sy, "ve", g["x_ve"])
        p.wait(sy, "pe", g["x_pe"])
        p.wait(sy, "se", g["x_se"])
        xld0 = p.cnt("ld")
        for k in range(DT_IN):
            p.inc(sy.dma_start(out=x_s[:, k * S:k * S + T],
                               in_=xin[k * 128:(k + 1) * 128, :]), "ld", 16)
        XLD = p.cnt("ld")
        p.wait(sc, "ld", XLD)
        p.wait(te, "ld", XLD)
        p.wait(dv, "ld", XLD)

        x_t = [x_s[:, k * S:k * S + T] for k in range(DT_IN)]
        x2_t = [x2_s[:, k * S:k * S + T] for k in range(DT_IN)]
        xn_t = [xn_s[:, k * S:k * S + T] for k in range(DT_IN)]
        emit_ln(p, te, sc, dv, x_t, x2_t, xn_t, DT_IN, T,
                ones_col, ones_row, rowbuf, S, ps_s1, ps_s2, ps_M, ps_V,
                neg_half, lnv_t, rstd_t)
        XN = p.cnt("ve")
        g["x_ve"], g["x_pe"], g["x_se"] = XN, p.cnt("pe"), p.cnt("se")

        p.wait(te, "ve", XN)
        nt = DT_U if mini else 2 * DT_U
        if not mini:
            p.wait(sc, "st", g["store_st"])
            p.wait(dv, "st", g["store_st"])
        for ot in range(nt):
            bank = ps_mm[ot % 2]
            p.wait(te, "se", g["bank_se"][ot % 2])
            for k in range(DT_IN):
                ins = te.matmul(
                    bank[:, 0:T],
                    w_in_s[:, k * 2 * DI + ot * 128: k * 2 * DI + ot * 128 + 128],
                    xn_s[:, k * S:k * S + T],
                    start=(k == 0), stop=(k == DT_IN - 1))
            p.inc(ins, "pe")
            p.wait(sc, "pe", p.cnt("pe"))
            if ot < DT_U:
                p.wait(sc, "ve", g["upre_halo_ve"])
                dst = upre[:, ot * S + 2: ot * S + 2 + T]
                p.inc(sc.activation(dst, bank[:, 0:T], AF.Identity,
                                    bias=b_in_s[:, ot:ot + 1]), "se")
            else:
                rt = ot - DT_U
                slot = rt % 2
                p.wait(sc, "st", g["res_st"][slot])
                p.inc(sc.activation(res_s[:, slot * TC: slot * TC + T],
                                    bank[:, 0:T], AF.Identity,
                                    bias=b_in_s[:, ot:ot + 1]), "se")
                p.wait(sy, "se", p.cnt("se"))
                p.inc(sy.dma_start(out=res_o[rt * 128:(rt + 1) * 128, t0:t0 + T],
                                   in_=res_s[:, slot * TC: slot * TC + T]),
                      "st", 16)
                g["res_st"][slot] = p.cnt("st")
            g["bank_se"][ot % 2] = p.cnt("se")
        WIN_SE = p.cnt("se")

        if mini:
            p.wait(dv, "se", WIN_SE)
            p.stall("dv")
            for ot in range(DT_U):
                dv.tensor_scalar_mul(upre[:, ot * S + 2:ot * S + 4],
                                     upre[:, ot * S + 2:ot * S + 4],
                                     hmask_s[:, 0:1])
            for ot in range(DT_U):
                ins = dv.tensor_copy(upre[:, ot * S:ot * S + 2],
                                     upre[:, ot * S + 2:ot * S + 4])
            p.inc(ins, "ve")
            g["upre_halo_ve"] = p.cnt("ve")
            continue

        p.wait(dv, "se", WIN_SE)
        for ot in range(DT_U):
            ub = upre[:, ot * S:(ot + 1) * S]
            uc = uconv_s[:, ot * TC:ot * TC + T]
            dv.tensor_scalar_mul(uc, ub[:, 0:T], convw_s[:, 3 * ot:3 * ot + 1])
            dv.scalar_tensor_tensor(uc, ub[:, 1:1 + T],
                                    convw_s[:, 3 * ot + 1:3 * ot + 2],
                                    uc, ALU.mult, ALU.add)
            dv.scalar_tensor_tensor(uc, ub[:, 2:2 + T],
                                    convw_s[:, 3 * ot + 2:3 * ot + 3],
                                    uc, ALU.mult, ALU.add)
            ins = dv.tensor_copy(upre[:, ot * S:ot * S + 2],
                                 upre[:, ot * S + T:ot * S + T + 2])
        p.inc(ins, "ve")
        CONV = p.cnt("ve")
        g["upre_halo_ve"] = CONV
        p.wait(sc, "ve", CONV)
        for ot in range(DT_U):
            ins = sc.activation(uconv_s[:, ot * TC:ot * TC + T],
                                uconv_s[:, ot * TC:ot * TC + T], AF.Silu,
                                bias=convb_s[:, ot:ot + 1])
        p.inc(ins, "se")
        SIL = p.cnt("se")
        p.wait(dv, "se", SIL)
        for ot in range(DT_U):
            ins = dv.tensor_copy(uconv_b[:, ot * TC:ot * TC + T],
                                 uconv_s[:, ot * TC:ot * TC + T])
        p.inc(ins, "ve")
        UB = p.cnt("ve")

        p.wait(te, "ve", UB)
        for k in range(DT_U):
            ins = te.matmul(ps_xp[0:96, 0:T], w_x_s[:, k * 96:(k + 1) * 96],
                            uconv_b[:, k * TC:k * TC + T],
                            start=(k == 0), stop=(k == DT_U - 1))
        p.inc(ins, "pe")
        p.wait(dv, "pe", p.cnt("pe"))
        p.inc(dv.tensor_copy(xp_s[0:96, 0:T], ps_xp[0:96, 0:T]), "ve")
        XPE = p.cnt("ve")

        p.wait(sy, "se", SIL)
        p.wait(sy, "ve", XPE)
        for ot in range(DT_U):
            p.inc(sy.dma_start(out=u_conv[ot * 128:(ot + 1) * 128, t0:t0 + T],
                               in_=uconv_s[:, ot * TC:ot * TC + T]), "st", 16)
        p.inc(sy.dma_start(out=xp_o[:, t0:t0 + T], in_=xp_s[0:96, 0:T]), "st", 16)
        g["store_st"] = p.cnt("st")

    p.wait(sy, "st")
    p.close()
    return p


def prep_L1(inputs):
    x = np.asarray(inputs["L0_attn_x"], np.float32)
    W_in = np.ascontiguousarray(np.asarray(inputs["L0_attn_W_in"], np.float32))
    b_in_t = t128(inputs["L0_attn_b_in"])
    cw = np.asarray(inputs["L0_attn_conv_w"], np.float32)[:, 0, :]
    convw_t = np.ascontiguousarray(
        cw.reshape(16, 128, 3).transpose(1, 0, 2).reshape(128, 48))
    conv_b_t = t128(inputs["L0_attn_conv_b"])
    W_x = np.ascontiguousarray(np.asarray(inputs["L0_attn_W_x"], np.float32))
    in_maps = []
    for c in range(8):
        b, h = c // 2, c % 2
        xbT = np.ascontiguousarray(x[b].T)
        x_fm = np.ascontiguousarray(xbT[:, h * LH:(h + 1) * LH])
        if h == 1:
            xhalo = np.ascontiguousarray(xbT[:, LH - 2:LH])
            hm = np.ones((128, 1), np.float32)
        else:
            xhalo = np.zeros((D, 2), np.float32)
            hm = np.zeros((128, 1), np.float32)
        in_maps.append(dict(x_fm=x_fm, xhalo=xhalo, hmask=hm, W_in=W_in,
                            b_in_t=b_in_t, convw_t=convw_t, conv_b_t=conv_b_t,
                            W_x=W_x))
    return in_maps


def np_ln(x, axis=-1):
    m = x.mean(axis, keepdims=True)
    v = x.var(axis, keepdims=True)
    return (x - m) / np.sqrt(v + EPS)


def np_silu(x):
    return x / (1.0 + np.exp(-x))


def build_L2(n_chunks=None, dbg=False):
    p = Prog()
    nc = p.nc
    u_i = p.din("u_s", [DIH, L])
    res_i = p.din("res_s", [DIH, L])
    xp_dt = p.din("xp_dt", [DTR, L])
    xp_B = p.din("xp_B", [NST, L])
    xp_C = p.din("xp_C", [NST, L])
    A_log_s = p.din("A_log_s", [DIH, NST])
    W_dt_s = p.din("W_dt_s", [DTR, DIH])
    b_dt_t = p.din("b_dt_t", [128, 8])
    W_out_s = p.din("W_out_s", [DIH, D])
    esel = p.din("esel", [NST, NST * 128])
    pout = p.dout("pout", [D, L])
    if dbg:
        dbg_delta = p.dout("dbg_delta", [DIH, 512])
        dbg_brep = p.dout("dbg_brep", [128, 2 * 512])
        dbg_state = p.dout("dbg_state", [DIH, NST])
        dbg_ym = p.dout("dbg_ym", [DIH, 512])

    T = 512
    NCH = L // T
    if n_chunks is not None:
        NCH = n_chunks
    NDT = DIH // 128

    for sname in ["ld", "uld", "pe", "ve", "se", "po", "st"]:
        p.sem(sname)
    p.init_fences(pool=True)
    sy, gp, te, dv, sc, po = (nc.sync, nc.gpsimd, nc.tensor, nc.vector,
                              nc.scalar, nc.gpsimd)

    esel_s = p.sbuf("esel_s", [NST, NST * 128], BF16)
    wdt_s = p.sbuf("wdt_s", [DTR, DIH], BF16)
    bdt_s = p.sbuf("bdt_s", [128, 8], F32)
    A_s = p.sbuf("A_s", [128, NDT * NST], F32)
    state = p.sbuf("state", [128, NDT * NST], F32)
    u_c1 = p.sbuf("u_c", [128, NDT * T], BF16)
    res_c2 = [p.sbuf(f"res_c{i}", [128, NDT * T], BF16) for i in range(2)]
    delta_c2 = [p.sbuf(f"delta_c{i}", [128, NDT * T], BF16) for i in range(2)]
    brep2 = [p.sbuf(f"brep{i}", [128, NST * T], BF16) for i in range(2)]
    du_b2 = [p.sbuf(f"du_b{i}", [128, NDT * T], BF16) for i in range(2)]
    crep2 = [p.sbuf(f"crep{i}", [128, NST * T], BF16) for i in range(2)]
    xpb_c = p.sbuf("xpb_c", [NST, T], BF16)
    xpc_c = p.sbuf("xpc_c", [NST, T], BF16)
    dtb_c = p.sbuf("dtb_c", [DTR, T], BF16)
    H = [p.sbuf("H0", [128, NST * T], BF16), p.sbuf("H1", [128, NST * T], BF16)]
    DA_all = p.sbuf("DA_all", [128, NST * T], BF16)
    DBU = [p.sbuf(f"DBU{i}", [128, T], BF16) for i in range(2)]
    wout_r = p.sbuf("wout_r", [128, NDT * D], BF16)
    ymb = p.sbuf("ymb", [128, NDT * T], BF16)
    pout_s = [p.sbuf("pout_s0", [128, T], F32)]

    ps_B = [p.psum("ps_B0", [128, T]), p.psum("ps_B1", [128, T])]
    ps_C = [p.psum("ps_C0", [128, T]), p.psum("ps_C1", [128, T])]
    ps_dl = [p.psum("ps_dl0", [128, T]), p.psum("ps_dl1", [128, T])]
    ps_w = [p.psum("ps_w0", [128, T]), p.psum("ps_w1", [128, T])]

    p.inc(gp.dma_start(out=esel_s, in_=esel), "ld", 16)
    p.inc(gp.dma_start(out=wdt_s, in_=W_dt_s), "ld", 16)
    for k in range(NDT):
        p.inc(gp.dma_start(out=wout_r[:, k * D:(k + 1) * D],
                           in_=W_out_s[k * 128:(k + 1) * 128, :]), "ld", 16)
    p.inc(sy.dma_start(out=bdt_s, in_=b_dt_t), "ld", 16)
    for k in range(NDT):
        p.inc(sy.dma_start(out=A_s[:, k * NST:(k + 1) * NST],
                           in_=A_log_s[k * 128:(k + 1) * 128, :]), "ld", 16)
    LD0 = p.cnt("ld")
    p.wait(sc, "ld", LD0)
    p.inc(sc.activation(A_s, A_s, AF.Exp), "se")
    p.fence_sc()
    p.wait(dv, "se")
    p.stall("dv")
    p.inc(dv.tensor_scalar_mul(A_s, A_s, -1.0), "ve")
    p.inc(dv.memset(state, 0.0), "ve")
    p.fence_dv()
    p.fence_dv()

    g = dict(u_ve=0, res_po=0, brep_ve=0, crep_po=0, delta_ve=0, delta_se=0,
             u_ve2=[0, 0], res_ve2=[0, 0], brep_ve2=[0, 0], crep_po2=[0, 0],
             crep_ve2=[0, 0],
             delta_ve2=[0, 0],
             psB_se=[0, 0], psC_se=[0, 0], psdl_se=[0, 0], psw_se=[0, 0],
             hpo=[0, 0],
             da_ve=[0, 0, 0], da_mid=0,
             wout_ld=[0, 0], pout_st=[0],
             xp_pe=0)

    for c in range(NCH):
        t0 = c * T
        pc = c % 2
        u_c, res_c, delta_c = u_c1, res_c2[pc], delta_c2[pc]
        brep, du_b, crep = brep2[pc], du_b2[pc], crep2[pc]
        p.wait(sy, "ve", g["u_ve"])
        p.wait(sy, "ve", g["res_ve2"][pc])
        p.wait(sy, "pe", g["xp_pe"])
        uld0 = p.cnt("uld")
        for k in range(NDT):
            p.inc(gp.dma_start(out=u_c[:, k * T:(k + 1) * T],
                               in_=u_i[k * 128:(k + 1) * 128, t0:t0 + T]),
                  "uld", 16)
            p.inc(gp.dma_start(out=res_c[:, k * T:(k + 1) * T],
                               in_=res_i[k * 128:(k + 1) * 128, t0:t0 + T]),
                  "uld", 16)
        p.inc(gp.dma_start(out=xpb_c, in_=xp_B[:, t0:t0 + T]), "uld", 16)
        p.inc(gp.dma_start(out=xpc_c, in_=xp_C[:, t0:t0 + T]), "uld", 16)
        p.wait(gp, "pe", g["xp_pe"])
        p.inc(gp.dma_start(out=dtb_c, in_=xp_dt[:, t0:t0 + T]), "uld", 16)
        ULD = p.cnt("uld")

        p.wait(te, "ld", LD0)
        p.wait(te, "uld", ULD)
        for n in range(NST):
            pb = n % 2
            p.wait(te, "se", g["psB_se"][pb])
            ins = te.matmul(ps_B[pb], esel_s[:, n * 128:(n + 1) * 128],
                            xpb_c, start=True, stop=True)
            p.inc(ins, "pe")
            p.wait(te, "se", g["psC_se"][pb])
            ins = te.matmul(ps_C[pb], esel_s[:, n * 128:(n + 1) * 128],
                            xpc_c, start=True, stop=True)
            p.inc(ins, "pe")
            p.wait(sc, "pe", p.cnt("pe"))
            if c > 1 and n == 0:
                p.wait(sc, "ve", g["brep_ve2"][pc])
                p.wait(sc, "po", g["crep_po2"][pc])
                p.wait(sc, "ve", g["crep_ve2"][pc])
            p.inc(sc.activation(brep[:, n * T:(n + 1) * T], ps_B[pb],
                                AF.Identity), "se")
            g["psB_se"][pb] = p.cnt("se")
            p.inc(sc.activation(crep[:, n * T:(n + 1) * T], ps_C[pb],
                                AF.Identity), "se")
            g["psC_se"][pb] = p.cnt("se")
        BREP_SE = p.cnt("se")

        for k in range(NDT):
            pb = k % 2
            p.wait(te, "se", g["psdl_se"][pb])
            ins = te.matmul(ps_dl[pb], wdt_s[:, k * 128:(k + 1) * 128],
                            dtb_c, start=True, stop=True)
            p.inc(ins, "pe")
            p.wait(sc, "pe", p.cnt("pe"))
            if c > 1 and k == 0:
                p.wait(sc, "ve", g["delta_ve2"][pc])
            dsl = delta_c[:, k * T:(k + 1) * T]
            sc.activation(dsl, ps_dl[pb], AF.Exp, bias=bdt_s[:, k:k + 1])
            p.inc(sc.activation(dsl, dsl, AF.Ln, bias=1.0), "se")
            g["psdl_se"][pb] = p.cnt("se")
        g["delta_se"] = p.cnt("se")
        for k in range(NDT):
            p.inc(sc.activation(res_c[:, k * T:(k + 1) * T],
                                res_c[:, k * T:(k + 1) * T], AF.Silu), "se")
        p.fence_sc()
        SRF = p.cnt("se")
        p.wait(dv, "se", SRF)
        p.stall("dv")
        for k in range(NDT):
            ins = dv.tensor_mul(du_b[:, k * T:(k + 1) * T],
                                delta_c[:, k * T:(k + 1) * T],
                                u_c[:, k * T:(k + 1) * T])
        p.inc(ins, "ve")
        g["u_ve"] = p.cnt("ve")

        def emit_tree(kk, POC):
            hs2 = H[kk % 2]
            p.wait(dv, "po", POC)
            p.stall("dv")
            p.stall("dv")
            dv.tensor_add(hs2[:, 0:8 * T], hs2[:, 0:8 * T],
                          hs2[:, 8 * T:16 * T])
            dv.tensor_add(hs2[:, 0:4 * T], hs2[:, 0:4 * T],
                          hs2[:, 4 * T:8 * T])
            dv.tensor_add(hs2[:, 0:2 * T], hs2[:, 0:2 * T],
                          hs2[:, 2 * T:4 * T])
            dv.tensor_add(hs2[:, 0:T], hs2[:, 0:T], hs2[:, T:2 * T])
            ins2 = dv.tensor_mul(ymb[:, kk * T:(kk + 1) * T], hs2[:, 0:T],
                                 res_c[:, kk * T:(kk + 1) * T])
            p.inc(ins2, "ve")

        pend = None
        for k in range(NDT):
            hs = H[k % 2]
            p.wait(sc, "ve", g["da_mid"])
            for n in range(NST):
                ins = sc.activation(DA_all[:, n * T:(n + 1) * T],
                                    delta_c[:, k * T:(k + 1) * T], AF.Exp,
                                    scale=A_s[:, k * NST + n:k * NST + n + 1])
            p.inc(ins, "se")
            p.fence_sc()
            DA_SE = p.cnt("se")
            p.wait(dv, "se", DA_SE)
            p.stall("dv")
            dv.tensor_mul(DBU[0], du_b[:, k * T:(k + 1) * T], brep[:, 0:T])
            for n in range(NST):
                if n + 1 < NST:
                    dv.tensor_mul(DBU[(n + 1) % 2],
                                  du_b[:, k * T:(k + 1) * T],
                                  brep[:, (n + 1) * T:(n + 2) * T])
                ins = dv.tensor_tensor_scan(
                    hs[:, n * T:(n + 1) * T], DA_all[:, n * T:(n + 1) * T],
                    DBU[n % 2], state[:, k * NST + n:k * NST + n + 1],
                    ALU.mult, ALU.add)
                p.inc(ins, "ve")
                if n == 8:
                    g["da_mid"] = p.cnt("ve")
            p.stall("dv")
            p.stall("dv")
            hv = hs.rearrange("p (n t) -> p n t", n=NST)
            p.inc(dv.tensor_copy(state[:, k * NST:(k + 1) * NST],
                                 hv[:, :, T - 1]), "ve")
            p.fence_dv()
            DV_K = p.cnt("ve")
            p.wait(po, "se", SRF)
            p.wait(po, "ve", DV_K)
            ins = po.tensor_mul(hs[:, 0:12 * T], hs[:, 0:12 * T],
                                crep[:, 0:12 * T])
            p.inc(ins, "po")
            p.fence_po()
            ins = dv.tensor_mul(hs[:, 12 * T:16 * T], hs[:, 12 * T:16 * T],
                                crep[:, 12 * T:16 * T])
            p.inc(ins, "ve")
            g["crep_ve2"][pc] = p.cnt("ve")
            if pend is not None:
                emit_tree(*pend)
            pend = (k, p.cnt("po"))
        emit_tree(*pend)
        g["res_ve2"][pc] = p.cnt("ve")
        g["crep_po2"][pc] = p.cnt("po")
        g["delta_ve2"][pc] = p.cnt("ve")
        g["brep_ve2"][pc] = p.cnt("ve")
        p.fence_dv()
        PO_F = p.cnt("ve")
        if dbg and c == 0:
            p.wait(sy, "se", p.cnt("se"))
            p.wait(sy, "ve", p.cnt("ve"))
            p.wait(sy, "po", PO_F)
            for k in range(NDT):
                p.inc(sy.dma_start(out=dbg_delta[k * 128:(k + 1) * 128, :],
                                   in_=delta_c[:, k * T:(k + 1) * T]), "st", 16)
                p.inc(sy.dma_start(out=dbg_state[k * 128:(k + 1) * 128, :],
                                   in_=state[:, k * NST:(k + 1) * NST]), "st", 16)
                p.inc(sy.dma_start(out=dbg_ym[k * 128:(k + 1) * 128, :],
                                   in_=res_c[:, k * T:(k + 1) * T]), "st", 16)
            p.inc(sy.dma_start(out=dbg_brep[:, 0:512], in_=brep[:, 0:T]), "st", 16)
            p.inc(sy.dma_start(out=dbg_brep[:, 512:1024],
                               in_=brep[:, 5 * T:6 * T]), "st", 16)
            DBG_ST = p.cnt("st")
            p.wait(sc, "st", DBG_ST)
            p.wait(dv, "st", DBG_ST)
            p.wait(po, "st", DBG_ST)

        p.wait(te, "ve", PO_F)
        p.wait(te, "ld", LD0)
        for ot in range(8):
            pw = ot % 2
            pq = 0
            p.wait(te, "se", g["psw_se"][pw])
            for k in range(NDT):
                ins = p.inc(te.matmul(ps_w[pw],
                                      wout_r[:, k * D + ot * 128:
                                             k * D + ot * 128 + 128],
                                      ymb[:, k * T:(k + 1) * T],
                                      start=(k == 0), stop=(k == NDT - 1)),
                            "pe")
            PE_W = p.cnt("pe")
            p.wait(sc, "pe", PE_W)
            p.wait(sc, "st", g["pout_st"][pq])
            p.inc(sc.activation(pout_s[pq], ps_w[pw], AF.Identity), "se")
            g["psw_se"][pw] = p.cnt("se")
            p.wait(sy, "se", p.cnt("se"))
            p.inc(sy.dma_start(out=pout[ot * 128:(ot + 1) * 128, t0:t0 + T],
                               in_=pout_s[pq]), "st", 16)
            g["pout_st"][pq] = p.cnt("st")
        g["xp_pe"] = p.cnt("pe")

    p.wait(sy, "st")
    p.close()
    return p


def prep_L2(inputs, l1_results):
    A_log = np.asarray(inputs["L0_attn_A_log"], np.float32)
    W_dt = np.asarray(inputs["L0_attn_W_dt"], np.float32)
    b_dt = np.asarray(inputs["L0_attn_b_dt"], np.float32)
    W_out = np.asarray(inputs["L0_attn_W_out"], np.float32)
    esel = np.zeros((NST, NST * 128), np.float32)
    for n in range(NST):
        esel[n, n * 128:(n + 1) * 128] = 1.0
    in_maps = []
    for c in range(8):
        b, dh = c // 2, c % 2
        u_full = np.concatenate([l1_results[2 * b]["u_conv"],
                                 l1_results[2 * b + 1]["u_conv"]], axis=1)
        res_full = np.concatenate([l1_results[2 * b]["res"],
                                   l1_results[2 * b + 1]["res"]], axis=1)
        xp_full = np.concatenate([l1_results[2 * b]["xp"],
                                  l1_results[2 * b + 1]["xp"]], axis=1)
        ds = slice(dh * DIH, (dh + 1) * DIH)
        in_maps.append(dict(
            u_s=np.ascontiguousarray(u_full[ds]),
            res_s=np.ascontiguousarray(res_full[ds]),
            xp_dt=np.ascontiguousarray(xp_full[0:DTR]),
            xp_B=np.ascontiguousarray(xp_full[DTR:DTR + NST]),
            xp_C=np.ascontiguousarray(xp_full[DTR + NST:DTR + 2 * NST]),
            A_log_s=np.ascontiguousarray(A_log[ds]),
            W_dt_s=np.ascontiguousarray(W_dt[:, ds]),
            b_dt_t=t128(b_dt[ds]),
            W_out_s=np.ascontiguousarray(W_out[ds]),
            esel=esel,
        ))
    return in_maps


def tilize(w, nk, no):
    t = w.reshape(nk, 128, no, 128).transpose(2, 1, 0, 3).reshape(no, 128, -1)
    return np.ascontiguousarray(t.astype(np.float32))


def build_L3():
    p = Prog()
    nc = p.nc
    x_fm = p.din("x_fm", [D, LH])
    p0 = p.din("p0", [D, LH])
    p1 = p.din("p1", [D, LH])
    b_out_t = p.din("b_out_t", [128, 8])
    w1T = p.din("w1T", [32, 128, 8 * 128])
    w2T = p.din("w2T", [32, 128, 8 * 128])
    h1T = p.din("h1T", [32, 128, 32 * 128])
    h2T = p.din("h2T", [32, 128, 32 * 128])
    w3T = p.din("w3T", [8, 128, 32 * 128])
    wvT = p.din("wvT", [D, D])
    w1s = p.din("w1s", [128, 32])
    w2s = p.din("w2s", [128, 32])
    h1s = p.din("h1s", [128, 32])
    h2s = p.din("h2s", [128, 32])
    w3s = p.din("w3s", [128, 8])
    x2_o = p.dout("x2", [D, LH])
    xn2_o = p.dout("xn2", [D, LH], BF16)
    v_o = p.dout("v_tm", [LH, D], BF16)
    hh_d = nc.dram_tensor("hh_d", [DH, LH], BF16, kind="Internal").ap()
    g1_d = nc.dram_tensor("g1_d", [DH, LH], BF16, kind="Internal").ap()
    g2_d = nc.dram_tensor("g2_d", [DH, LH], BF16, kind="Internal").ap()
    x1_d = nc.dram_tensor("x1_d", [D, LH], F32, kind="Internal").ap()

    T = 512
    NTC = LH // T
    ND = D // 128

    for sname in ["ld", "wld", "pe", "ve", "se", "st"]:
        p.sem(sname)
    p.init_fences()
    sy, gp, te, dv, sc = nc.sync, nc.gpsimd, nc.tensor, nc.vector, nc.scalar

    ones_col = p.sbuf("ones_col", [128, 1], F32)
    ones_row = p.sbuf("ones_row", [1, 128], F32)
    neg_half = p.sbuf("neg_half", [128, 1], F32)
    ps_s1 = p.psum("ps_s1", [128, T])
    ps_s2 = p.psum("ps_s2", [1, T])
    ps_M = p.psum("ps_M", [128, T])
    ps_V = p.psum("ps_V", [128, T])
    ps_a = [p.psum("ps_a0", [128, T]), p.psum("ps_a1", [128, T])]
    ps_b = [p.psum("ps_b0", [128, T]), p.psum("ps_b1", [128, T])]
    rowbuf = p.sbuf("rowbuf", [1, 2 * T], F32)
    rstd_t = p.sbuf("rstd_t", [128, T], F32)
    lnv_t = ps_s1
    scal = p.sbuf("scal", [128, 4 * 32 + 8 + 8], F32)
    RB = p.sbuf("RB", [128, 65536], BF16)
    SCR = p.sbuf("SCR", [128, 8192], F32)
    ab_t = p.sbuf("ab_t", [128, 4 * T], F32)
    outb = p.sbuf("outb", [128, 4 * T], BF16)
    xn2_c = p.sbuf("xn2_c", [128, 8 * T], BF16)
    wbuf = [p.sbuf(f"wbuf{i}", [128, 32 * 128], BF16) for i in range(2)]
    wbufB = [p.sbuf(f"wbufB{i}", [128, 8 * 128], BF16) for i in range(2)]


    OW1, OW2, OH1, OH2, OW3, OBO = 0, 32, 64, 96, 128, 136

    p.inc(sy.dma_start(out=scal[:, OW1:OW1 + 32], in_=w1s), "ld", 16)
    p.inc(sy.dma_start(out=scal[:, OW2:OW2 + 32], in_=w2s), "ld", 16)
    p.inc(sy.dma_start(out=scal[:, OH1:OH1 + 32], in_=h1s), "ld", 16)
    p.inc(sy.dma_start(out=scal[:, OH2:OH2 + 32], in_=h2s), "ld", 16)
    p.inc(sy.dma_start(out=scal[:, OW3:OW3 + 8], in_=w3s), "ld", 16)
    p.inc(sy.dma_start(out=scal[:, OBO:OBO + 8], in_=b_out_t), "ld", 16)
    LD0 = p.cnt("ld")
    p.wait(dv, "ld", LD0)
    p.inc(dv.memset(ones_col, 1.0), "ve")
    p.inc(dv.memset(ones_row, 1.0), "ve")
    p.inc(dv.memset(neg_half, -0.5), "ve")
    p.fence_dv()

    g = dict(psa_se=[0, 0], psb_se=[0, 0], ab_ve=[0, 0], out_st=[0, 0, 0, 0],
             w_pe=[0, 0], w2_pe=[0, 0], scr_ve=0, scr_pe=0, rb_pe=0,
             xn2_st=0)

    xv = SCR[:, 0:8 * T]
    pv = SCR[:, 8 * T:16 * T]
    qv = xv

    def ln_chunk(x_tiles, x2_tiles, xn_tiles):
        emit_ln(p, te, sc, dv, x_tiles, x2_tiles, xn_tiles, ND, T,
                ones_col, ones_row, rowbuf, T, ps_s1, ps_s2, ps_M, ps_V,
                neg_half, lnv_t, rstd_t, ms_row=rstd_t[0:1, 0:T])

    for tc in range(NTC):
        t0 = tc * T
        p.wait(sy, "ve", g["scr_ve"])
        p.wait(sy, "pe", g["scr_pe"])
        for k in range(ND):
            p.inc(sy.dma_start(out=pv[:, k * T:(k + 1) * T],
                               in_=p0[k * 128:(k + 1) * 128, t0:t0 + T]),
                  "ld", 16)
            p.inc(sy.dma_start(out=xv[:, k * T:(k + 1) * T],
                               in_=p1[k * 128:(k + 1) * 128, t0:t0 + T]),
                  "ld", 16)
        LDC = p.cnt("ld")
        p.wait(dv, "ld", LDC)
        for k in range(ND):
            sl = slice(k * T, (k + 1) * T)
            ins = dv.tensor_add(pv[:, sl], pv[:, sl], xv[:, sl])
        p.inc(ins, "ve")
        p.fence_dv()
        p.wait(sy, "ve", p.cnt("ve"))
        for k in range(ND):
            p.inc(sy.dma_start(out=xv[:, k * T:(k + 1) * T],
                               in_=x_fm[k * 128:(k + 1) * 128, t0:t0 + T]),
                  "ld", 16)
        LDC = p.cnt("ld")
        p.wait(dv, "ld", LDC)
        for k in range(ND):
            sl = slice(k * T, (k + 1) * T)
            ins = dv.tensor_add(pv[:, sl], pv[:, sl], xv[:, sl])
        p.inc(ins, "ve")
        p.fence_dv()
        p.wait(sc, "ve", p.cnt("ve"))
        for k in range(ND):
            sl = slice(k * T, (k + 1) * T)
            ins = sc.activation(pv[:, sl], pv[:, sl], AF.Identity,
                                bias=scal[:, OBO + k:OBO + k + 1])
        p.inc(ins, "se")
        p.fence_sc()
        SE_X1 = p.cnt("se")
        p.wait(sy, "se", SE_X1)
        for k in range(ND):
            p.inc(sy.dma_start(out=x1_d[k * 128:(k + 1) * 128, t0:t0 + T],
                               in_=pv[:, k * T:(k + 1) * T]), "st", 16)
        p.wait(dv, "se", SE_X1)
        p.stall("dv")
        x_t = [pv[:, k * T:(k + 1) * T] for k in range(ND)]
        x2_t = [qv[:, k * T:(k + 1) * T] for k in range(ND)]
        xn_t = [RB[:, k * 2048 + t0:k * 2048 + t0 + T] for k in range(ND)]
        ln_chunk(x_t, x2_t, xn_t)
        g["scr_ve"] = p.cnt("ve")
        g["scr_pe"] = p.cnt("pe")
    p.fence_dv()
    XN1_VE = p.cnt("ve")

    def stream_w(slab_ap, ot, buf, nk, which):
        p.wait(gp, "pe", g[which][ot % 2])
        p.inc(gp.dma_start(out=buf[:, 0:nk * 128], in_=slab_ap[ot]),
              "wld", 16)
        return p.cnt("wld")

    for ot in range(32):
        wa = wbuf[ot % 2]
        wb2 = wbufB[ot % 2]
        W1LD = stream_w(w1T, ot, wa, 8, "w_pe")
        W2LD = stream_w(w2T, ot, wb2, 8, "w2_pe")
        for tc in range(NTC):
            t0 = tc * T
            pa, pb = ps_a[tc % 2], ps_b[tc % 2]
            p.wait(te, "wld", W1LD)
            p.wait(te, "wld", W2LD)
            if ot == 0 and tc == 0:
                p.wait(te, "ve", XN1_VE)
            p.wait(te, "se", g["psa_se"][tc % 2])
            for k in range(ND):
                ins = te.matmul(pa, wa[:, k * 128:(k + 1) * 128],
                                RB[:, k * 2048 + t0:k * 2048 + t0 + T],
                                start=(k == 0), stop=(k == ND - 1))
            p.inc(ins, "pe")
            g["w_pe"][ot % 2] = p.cnt("pe")
            p.wait(te, "se", g["psb_se"][tc % 2])
            for k in range(ND):
                ins = te.matmul(pb, wb2[:, k * 128:(k + 1) * 128],
                                RB[:, k * 2048 + t0:k * 2048 + t0 + T],
                                start=(k == 0), stop=(k == ND - 1))
            p.inc(ins, "pe")
            g["w2_pe"][ot % 2] = p.cnt("pe")
            PEC = p.cnt("pe")
            p.wait(sc, "pe", PEC)
            p.wait(sc, "ve", g["ab_ve"][tc % 2])
            aa = ab_t[:, (tc % 2) * 2 * T:(tc % 2) * 2 * T + T]
            bb = ab_t[:, (tc % 2) * 2 * T + T:(tc % 2) * 2 * T + 2 * T]
            sc.activation(aa, pa, AF.Silu, scale=scal[:, OW1 + ot:OW1 + ot + 1])
            p.inc(sc.activation(bb, pb, AF.Identity,
                                scale=scal[:, OW2 + ot:OW2 + ot + 1]), "se")
            g["psa_se"][tc % 2] = p.cnt("se")
            g["psb_se"][tc % 2] = p.cnt("se")
            p.fence_sc()
            p.wait(dv, "se", p.cnt("se"))
            ob = outb[:, (tc % 2) * T:(tc % 2) * T + T]
            p.wait(dv, "st", g["out_st"][tc % 2])
            p.inc(dv.tensor_mul(ob, aa, bb), "ve")
            g["ab_ve"][tc % 2] = p.cnt("ve")
            p.fence_dv()
            p.wait(sy, "ve", p.cnt("ve"))
            p.inc(sy.dma_start(out=hh_d[ot * 128:(ot + 1) * 128, t0:t0 + T],
                               in_=ob), "st", 16)
            g["out_st"][tc % 2] = p.cnt("st")

    def big_stage(wT_ap, src_d, dst_d, scal_off, func, nk, no, dst_x1=False):
        p.wait(sy, "pe", p.cnt("pe"))
        p.wait(sy, "ve", p.cnt("ve"))
        p.wait(sy, "st", p.cnt("st"))
        for k in range(nk):
            p.inc(sy.dma_start(out=RB[:, k * 2048:(k + 1) * 2048],
                               in_=src_d[k * 128:(k + 1) * 128, :]), "ld", 16)
        RLD = p.cnt("ld")
        p.wait(te, "ld", RLD)
        for ot in range(no):
            wa = wbuf[ot % 2]
            WLD = stream_w(wT_ap, ot, wa, nk, "w_pe")
            for tc in range(NTC):
                t0 = tc * T
                pa = ps_a[tc % 2]
                p.wait(te, "wld", WLD)
                p.wait(te, "se", g["psa_se"][tc % 2])
                for k in range(nk):
                    ins = te.matmul(pa, wa[:, k * 128:(k + 1) * 128],
                                    RB[:, k * 2048 + t0:k * 2048 + t0 + T],
                                    start=(k == 0), stop=(k == nk - 1))
                p.inc(ins, "pe")
                g["w_pe"][ot % 2] = p.cnt("pe")
                PEC = p.cnt("pe")
                p.wait(sc, "pe", PEC)
                ob = outb[:, (2 + tc % 2) * T:(2 + tc % 2) * T + T]
                if dst_x1:
                    aa = ab_t[:, (tc % 2) * 2 * T:(tc % 2) * 2 * T + T]
                    p.wait(sc, "ve", g["ab_ve"][tc % 2])
                    p.inc(sc.activation(aa, pa, AF.Identity,
                                        scale=scal[:, scal_off + ot:
                                                   scal_off + ot + 1]), "se")
                    g["psa_se"][tc % 2] = p.cnt("se")
                else:
                    p.wait(sc, "st", g["out_st"][2 + tc % 2])
                    p.inc(sc.activation(ob, pa, func,
                                        scale=scal[:, scal_off + ot:
                                                   scal_off + ot + 1]), "se")
                    g["psa_se"][tc % 2] = p.cnt("se")
                    p.fence_sc()
                    p.wait(sy, "se", p.cnt("se"))
                    p.inc(sy.dma_start(
                        out=dst_d[ot * 128:(ot + 1) * 128, t0:t0 + T],
                        in_=ob), "st", 16)
                    g["out_st"][2 + tc % 2] = p.cnt("st")
                if dst_x1:
                    yield (ot, tc)

    for _ in big_stage(h1T, hh_d, g1_d, OH1, AF.Gelu, 32, 32):
        pass
    for _ in big_stage(h2T, g1_d, g2_d, OH2, AF.Gelu, 32, 32):
        pass

    p.wait(sy, "pe", p.cnt("pe"))
    p.wait(sy, "ve", p.cnt("ve"))
    p.wait(sy, "st", p.cnt("st"))
    for k in range(32):
        p.inc(sy.dma_start(out=RB[:, k * 2048:(k + 1) * 2048],
                           in_=g2_d[k * 128:(k + 1) * 128, :]), "ld", 16)
    RLD = p.cnt("ld")
    p.wait(te, "ld", RLD)
    wv_ld = 0
    for tc in range(NTC):
        t0 = tc * T
        p.wait(sy, "ve", g["scr_ve"])
        p.wait(sy, "pe", g["scr_pe"])
        for k in range(ND):
            p.inc(sy.dma_start(out=xv[:, k * T:(k + 1) * T],
                               in_=x1_d[k * 128:(k + 1) * 128, t0:t0 + T]),
                  "ld", 16)
        X1LD = p.cnt("ld")
        for ot in range(ND):
            wa = wbuf[ot % 2]
            WLD = stream_w(w3T, ot, wa, 32, "w_pe")
            pa = ps_a[ot % 2]
            p.wait(te, "wld", WLD)
            p.wait(te, "se", g["psa_se"][ot % 2])
            for k in range(32):
                ins = te.matmul(pa, wa[:, k * 128:(k + 1) * 128],
                                RB[:, k * 2048 + t0:k * 2048 + t0 + T],
                                start=(k == 0), stop=(k == 31))
            p.inc(ins, "pe")
            g["w_pe"][ot % 2] = p.cnt("pe")
            PEC = p.cnt("pe")
            p.wait(sc, "pe", PEC)
            aa = ab_t[:, (ot % 2) * 2 * T:(ot % 2) * 2 * T + T]
            p.wait(sc, "ve", g["ab_ve"][ot % 2])
            p.inc(sc.activation(aa, pa, AF.Identity,
                                scale=scal[:, OW3 + ot:OW3 + ot + 1]), "se")
            g["psa_se"][ot % 2] = p.cnt("se")
            p.fence_sc()
            p.wait(dv, "se", p.cnt("se"))
            p.wait(dv, "ld", X1LD)
            ins = dv.tensor_add(pv[:, ot * T:(ot + 1) * T], aa,
                                xv[:, ot * T:(ot + 1) * T])
            p.inc(ins, "ve")
            g["ab_ve"][ot % 2] = p.cnt("ve")
        p.fence_dv()
        X2VE = p.cnt("ve")
        p.wait(sy, "ve", X2VE)
        for k in range(ND):
            p.inc(sy.dma_start(out=x2_o[k * 128:(k + 1) * 128, t0:t0 + T],
                               in_=pv[:, k * T:(k + 1) * T]), "st", 16)
        x_t = [pv[:, k * T:(k + 1) * T] for k in range(ND)]
        x2_t = [qv[:, k * T:(k + 1) * T] for k in range(ND)]
        xn_t = [xn2_c[:, k * T:(k + 1) * T] for k in range(ND)]
        p.wait(dv, "st", g["xn2_st"])
        p.wait(dv, "pe", g["scr_pe"])
        ln_chunk(x_t, x2_t, xn_t)
        XN2VE = p.cnt("ve")
        p.wait(sy, "ve", XN2VE)
        for k in range(ND):
            p.inc(sy.dma_start(out=xn2_o[k * 128:(k + 1) * 128, t0:t0 + T],
                               in_=xn2_c[:, k * T:(k + 1) * T]), "st", 16)
        p.wait(gp, "pe", g["w_pe"][0])
        p.wait(gp, "pe", g["w_pe"][1])
        for eh in range(2):
            for k in range(ND):
                p.inc(gp.dma_start(
                    out=wbuf[eh][:, k * 512:(k + 1) * 512],
                    in_=wvT[k * 128:(k + 1) * 128,
                            eh * 512:(eh + 1) * 512]), "wld", 16)
        wv_ld = p.cnt("wld")
        for sub in range(T // 128):
            s0 = t0 + sub * 128
            for eh in range(2):
                pa = ps_b[eh]
                p.wait(te, "wld", wv_ld)
                p.wait(te, "ve", XN2VE)
                p.wait(te, "se", g["psb_se"][eh])
                for k in range(ND):
                    ins = te.matmul(
                        pa, xn2_c[:, k * T + sub * 128:k * T + sub * 128 + 128],
                        wbuf[eh][:, k * 512:(k + 1) * 512],
                        start=(k == 0), stop=(k == ND - 1))
                p.inc(ins, "pe")
                PEC = p.cnt("pe")
                p.wait(sc, "pe", PEC)
                ob = outb[:, (2 + eh) * T:(3 + eh) * T]
                p.wait(sc, "st", g["out_st"][2 + eh])
                p.inc(sc.activation(ob, pa, AF.Identity), "se")
                g["psb_se"][eh] = p.cnt("se")
                p.fence_sc()
                p.wait(sy, "se", p.cnt("se"))
                p.inc(sy.dma_start(out=v_o[s0:s0 + 128,
                                           eh * 512:(eh + 1) * 512],
                                   in_=ob), "st", 16)
                g["out_st"][2 + eh] = p.cnt("st")
        g["xn2_st"] = p.cnt("st")
        g["scr_ve"] = p.cnt("ve")
        g["scr_pe"] = p.cnt("pe")
        g["w_pe"][0] = p.cnt("pe")
        g["w_pe"][1] = p.cnt("pe")

    p.wait(sy, "st")
    p.close()
    return p


def prep_L3(inputs, l2_results):
    x = np.asarray(inputs["L0_attn_x"], np.float32)
    b_out_t = t128(inputs["L0_attn_b_out"])
    w1T = tilize(np.asarray(inputs["L0_ffn_w1_w"], np.float32).T, 8, 32)
    w2T = tilize(np.asarray(inputs["L0_ffn_w2_w"], np.float32).T, 8, 32)
    h1T = tilize(np.asarray(inputs["L0_ffn_h1_w"], np.float32).T, 32, 32)
    h2T = tilize(np.asarray(inputs["L0_ffn_h2_w"], np.float32).T, 32, 32)
    w3T = tilize(np.asarray(inputs["L0_ffn_w3_w"], np.float32).T, 32, 8)
    wvT = np.ascontiguousarray(np.asarray(inputs["L1_attn_W_v"], np.float32).T)
    w1s = t128(inputs["L0_ffn_w1_s"])
    w2s = t128(inputs["L0_ffn_w2_s"])
    h1s = t128(inputs["L0_ffn_h1_s"])
    h2s = t128(inputs["L0_ffn_h2_s"])
    w3s = t128(inputs["L0_ffn_w3_s"])
    in_maps = []
    for c in range(8):
        b, h = c // 2, c % 2
        sl = slice(h * LH, (h + 1) * LH)
        in_maps.append(dict(
            x_fm=np.ascontiguousarray(x[b].T[:, sl]),
            p0=np.ascontiguousarray(l2_results[2 * b]["pout"][:, sl]),
            p1=np.ascontiguousarray(l2_results[2 * b + 1]["pout"][:, sl]),
            b_out_t=b_out_t, w1T=w1T, w2T=w2T, h1T=h1T, h2T=h2T, w3T=w3T,
            wvT=wvT, w1s=w1s, w2s=w2s, h1s=h1s, h2s=h2s, w3s=w3s,
        ))
    return in_maps


def build_L4(dbg=False):
    p = Prog()
    nc = p.nc
    xn_f = p.din("xn_f", [D, L], BF16)
    v_f = p.din("v_f", [L, D], BF16)
    wqk = p.din("wqk", [D, D])
    slope = p.din("slope", [128, 1])
    diff_c = p.din("diff_c", [128, 512])
    maskv = p.din("maskv", [128, 4 * 512])
    maskf = p.din("maskf", [128, 512])
    out_o = p.dout("attn_out", [16, D, 128])
    if dbg:
        dbg_q = p.dout("dbg_q", [128, 8 * 128], BF16)
        dbg_sco = p.dout("dbg_sco", [128, 512])
        dbg_rm = p.dout("dbg_rm", [128, 2])
        dbg_at = p.dout("dbg_at", [128, 512], BF16)
        dbg_z = p.dout("dbg_z", [128, D])

    ND = 8
    NKT = 32

    for sname in ["ld", "pe", "ve", "se", "st"]:
        p.sem(sname)
    p.init_fences()
    sy, gp, te, dv, sc = nc.sync, nc.gpsimd, nc.tensor, nc.vector, nc.scalar

    K_s = p.sbuf("K_s", [128, ND * L], BF16)
    V_s = p.sbuf("V_s", [128, NKT * D], BF16)
    wqk_s = p.sbuf("wqk_s", [128, ND * D], BF16)
    Q_s = p.sbuf("Q_s", [128, ND * 128], BF16)
    sco = p.sbuf("sco", [128, 8 * 512], F32)
    attn = p.sbuf("attn", [128, 8 * 512], BF16)
    attnT = p.sbuf("attnT", [128, NKT * 128], BF16)
    z_sb = p.sbuf("z_sb", [128, D], F32)
    outf = p.sbuf("outf", [128, ND * 128], F32)
    bias_t = p.sbuf("bias_t", [128, 512], F32)
    diff_s = p.sbuf("diff_s", [128, 512], F32)
    mask_s = p.sbuf("mask_s", [128, 4 * 512], F32)
    maskf_s = p.sbuf("maskf_s", [128, 512], F32)
    slope_s = p.sbuf("slope_s", [128, 1], F32)
    rowm = p.sbuf("rowm", [128, 2], F32)

    ps_q = p.psum("ps_q", [128, 128])
    ps_s = [p.psum("ps_s0", [128, 512]), p.psum("ps_s1", [128, 512])]
    ps_t = [p.psum("ps_t0", [128, 128], BF16), p.psum("ps_tb", [128, 128], BF16)]
    ps_z = [p.psum("ps_z0", [128, 512]), p.psum("ps_z1", [128, 512])]
    ps_t2 = p.psum("ps_t2", [128, 128])
    ident = p.sbuf("ident", [128, 128], BF16)
    identf = p.sbuf("identf", [128, 128], F32)
    ident_i = p.din("ident_i", [128, 128])

    for k in range(ND):
        p.inc(sy.dma_start(out=K_s[:, k * L:(k + 1) * L],
                           in_=xn_f[k * 128:(k + 1) * 128, :]), "ld", 16)
        p.inc(gp.dma_start(out=wqk_s[:, k * D:(k + 1) * D],
                           in_=wqk[k * 128:(k + 1) * 128, :]), "ld", 16)
    for k in range(NKT):
        p.inc(sy.dma_start(out=V_s[:, k * D:(k + 1) * D],
                           in_=v_f[k * 128:(k + 1) * 128, :]), "ld", 16)
    p.inc(sy.dma_start(out=diff_s, in_=diff_c), "ld", 16)
    p.inc(sy.dma_start(out=mask_s, in_=maskv), "ld", 16)
    p.inc(sy.dma_start(out=maskf_s, in_=maskf), "ld", 16)
    p.inc(sy.dma_start(out=slope_s, in_=slope), "ld", 16)
    p.inc(gp.dma_start(out=ident, in_=ident_i), "ld", 16)
    p.inc(sy.dma_start(out=identf, in_=ident_i), "ld", 16)
    LD0 = p.cnt("ld")
    p.wait(te, "ld", LD0)
    p.wait(dv, "ld", LD0)
    p.wait(sc, "ld", LD0)

    g = dict(q_se=0, s_se=[0, 0], sco_dv=[0, 0], attn_pe=0, at_se=0,
             z_se=[0, 0], zsb_pe=0, out_st=0, t2_se=0, pst_se=[0, 0])

    for qi in range(16):
        qg = 2 * qi + 1
        q0v = qg * 128
        klen = q0v + 128
        nkc = (klen + 511) // 512
        nkt = klen // 128

        for et in range(ND):
            p.wait(te, "se", g["q_se"])
            for k in range(ND):
                ins = te.matmul(ps_q,
                                wqk_s[:, k * D + et * 128:k * D + et * 128 + 128],
                                K_s[:, k * L + q0v:k * L + q0v + 128],
                                start=(k == 0), stop=(k == ND - 1))
            p.inc(ins, "pe")
            p.wait(sc, "pe", p.cnt("pe"))
            p.inc(sc.activation(Q_s[:, et * 128:(et + 1) * 128], ps_q,
                                AF.Identity), "se")
            g["q_se"] = p.cnt("se")
        p.fence_sc()
        QSE = p.cnt("se")

        p.wait(te, "se", QSE)
        for kc in range(nkc):
            pb = kc % 2
            k0 = kc * 512
            p.wait(te, "ve", g["sco_dv"][pb])
            for et in range(ND):
                ins = te.matmul(ps_s[pb], Q_s[:, et * 128:(et + 1) * 128],
                                K_s[:, et * L + k0:et * L + k0 + 512],
                                start=(et == 0), stop=(et == ND - 1))
            p.inc(ins, "pe")
            p.wait(dv, "pe", p.cnt("pe"))
            dv.tensor_scalar(bias_t, diff_s, float(k0 - q0v), slope_s[:, 0:1],
                             ALU.add, ALU.mult)
            sl = sco[:, kc * 512:(kc + 1) * 512]
            dv.scalar_tensor_tensor(sl, ps_s[pb], 1.0 / np.sqrt(D), bias_t,
                                    ALU.mult, ALU.add)
            ins = None
            if kc == nkc - 1:
                off = (q0v - k0) // 128
                ins = dv.tensor_add(sl, sl,
                                    mask_s[:, off * 512:(off + 1) * 512])
            if kc == 0:
                ins = dv.tensor_add(sl, sl, maskf_s)
            if ins is None:
                ins = dv.tensor_copy(rowm[:, 0:1], rowm[:, 0:1])
            p.inc(ins, "ve")
            g["sco_dv"][pb] = p.cnt("ve")
        p.inc(dv.tensor_reduce(rowm[:, 0:1], sco[:, 0:nkc * 512], AX.X,
                               ALU.max, negate=True), "ve")
        p.fence_dv()
        p.fence_dv()
        p.wait(sc, "ve", p.cnt("ve"))
        p.stall("sc")
        for kc in range(nkc):
            sl = sco[:, kc * 512:(kc + 1) * 512]
            ins = sc.activation(sl, sl, AF.Exp, bias=rowm[:, 0:1])
        p.inc(ins, "se")
        p.fence_sc()
        p.wait(dv, "se", p.cnt("se"))
        p.stall("dv")
        p.inc(dv.tensor_reduce(rowm[:, 1:2], sco[:, 0:nkc * 512], AX.X,
                               ALU.add), "ve")
        for _ in range(4):
            p.stall("dv")
        p.inc(dv.reciprocal(rowm[:, 0:1], rowm[:, 1:2]), "ve")
        for _ in range(4):
            p.stall("dv")
        p.wait(dv, "pe", g["attn_pe"])
        for kc in range(nkc):
            ins = dv.tensor_scalar_mul(attn[:, kc * 512:(kc + 1) * 512],
                                       sco[:, kc * 512:(kc + 1) * 512],
                                       rowm[:, 0:1])
        p.inc(ins, "ve")
        p.fence_dv()
        ATTN_VE = p.cnt("ve")

        p.wait(te, "ve", ATTN_VE)
        p.wait(te, "se", g["at_se"])
        for kt in range(nkt):
            p.wait(te, "se", g["pst_se"][kt % 2])
            ins = te.transpose(ps_t[kt % 2], attn[:, kt * 128:(kt + 1) * 128],
                               ident)
            p.inc(ins, "pe")
            p.wait(sc, "pe", p.cnt("pe"))
            p.inc(sc.activation(attnT[:, kt * 128:(kt + 1) * 128],
                                ps_t[kt % 2], AF.Identity), "se")
            g["pst_se"][kt % 2] = p.cnt("se")
        p.fence_sc()
        ATSE = p.cnt("se")
        g["at_se"] = ATSE

        p.wait(te, "se", ATSE)
        for eh in range(2):
            p.wait(te, "se", g["z_se"][eh])
            for kt in range(nkt):
                ins = te.matmul(ps_z[eh], attnT[:, kt * 128:(kt + 1) * 128],
                                V_s[:, kt * D + eh * 512:kt * D + eh * 512 + 512],
                                start=(kt == 0), stop=(kt == nkt - 1))
            p.inc(ins, "pe")
            p.wait(sc, "pe", p.cnt("pe"))
            p.wait(sc, "pe", g["zsb_pe"])
            p.inc(sc.activation(z_sb[:, eh * 512:(eh + 1) * 512], ps_z[eh],
                                AF.Identity), "se")
            g["z_se"][eh] = p.cnt("se")
        p.fence_sc()
        ZSE = p.cnt("se")

        p.wait(te, "se", ZSE)
        for et in range(ND):
            p.wait(te, "se", g["t2_se"])
            ins = te.transpose(ps_t2, z_sb[:, et * 128:(et + 1) * 128], identf)
            p.inc(ins, "pe")
            p.wait(sc, "pe", p.cnt("pe"))
            p.wait(sc, "st", g["out_st"])
            p.inc(sc.activation(outf[:, et * 128:(et + 1) * 128], ps_t2,
                                AF.Identity), "se")
            g["t2_se"] = p.cnt("se")
        g["zsb_pe"] = p.cnt("pe")
        p.fence_sc()
        p.wait(sy, "se", p.cnt("se"))
        for et in range(ND):
            p.inc(sy.dma_start(out=out_o[qi, et * 128:(et + 1) * 128, :],
                               in_=outf[:, et * 128:(et + 1) * 128]), "st", 16)
        g["out_st"] = p.cnt("st")
        if dbg and qi == 0:
            p.wait(sy, "ve", p.cnt("ve"))
            p.inc(sy.dma_start(out=dbg_q, in_=Q_s), "st", 16)
            p.inc(sy.dma_start(out=dbg_sco, in_=sco[:, 0:512]), "st", 16)
            p.inc(sy.dma_start(out=dbg_rm, in_=rowm), "st", 16)
            p.inc(sy.dma_start(out=dbg_at, in_=attn[:, 0:512]), "st", 16)
            p.inc(sy.dma_start(out=dbg_z, in_=z_sb), "st", 16)
            DB = p.cnt("st")
            p.wait(te, "st", DB)
            p.wait(dv, "st", DB)
            p.wait(sc, "st", DB)
    p.wait(sy, "st")
    p.close()
    return p


def prep_L4(inputs, l3_results):
    mk_D, L_ = D, L
    Wqk = np.ascontiguousarray(np.asarray(inputs["L1_attn_W_qk"], np.float32))
    sl = float(np.asarray(inputs["L1_attn_alibi_slope"], np.float32)[0])
    slope = np.full((128, 1), sl, np.float32)
    dq = np.arange(128, dtype=np.float32)[:, None]
    dk = np.arange(512, dtype=np.float32)[None, :]
    diff_c = np.ascontiguousarray(np.broadcast_to(dk - dq, (128, 512)))
    ident = np.eye(128, dtype=np.float32)
    in_maps = []
    for c in range(8):
        b, par = c // 2, c % 2
        xn = np.concatenate([np.asarray(l3_results[2 * b]["xn2"]),
                             np.asarray(l3_results[2 * b + 1]["xn2"])], axis=1)
        v = np.concatenate([np.asarray(l3_results[2 * b]["v_tm"]),
                            np.asarray(l3_results[2 * b + 1]["v_tm"])], axis=0)
        if par == 0:
            xn = np.concatenate([np.zeros((mk_D, 128), xn.dtype),
                                 xn[:, :L_ - 128]], axis=1)
            v = np.concatenate([np.zeros((128, mk_D), v.dtype),
                                v[:L_ - 128]], axis=0)
        maskv = np.zeros((128, 4 * 512), np.float32)
        for off in range(4):
            m = (dk <= off * 128 + dq)
            maskv[:, off * 512:(off + 1) * 512] = np.where(m, 0.0, -1e30)
        maskf = np.where(dk < (128 if par == 0 else 0) + 0 * dq, -1e30,
                         0.0).astype(np.float32)
        maskf = np.broadcast_to(maskf, (128, 512)).copy()
        in_maps.append(dict(xn_f=np.ascontiguousarray(xn),
                            v_f=np.ascontiguousarray(v), wqk=Wqk, slope=slope,
                            diff_c=diff_c, maskv=maskv, maskf=maskf,
                            ident_i=ident))
    return in_maps


def build_L5(n_experts=E, dbg=False):
    p = Prog()
    nc = p.nc
    x2_i = p.din("x2_i", [D, LH])
    at_i = p.din("at_i", [D, LH])
    w1T = p.din("w1T", [E, 32, 128, 8 * 128])
    w2T = p.din("w2T", [E, 8, 128, 32 * 128])
    wrt = p.din("wrt", [128, 8 * E])
    esel8 = p.din("esel8", [E, E * 128])
    id_i = p.din("id_i", [128, 128])
    out_o = p.dout("out_fm", [D, LH])
    if dbg:
        dbg_xn3 = p.dout("dbg_xn3", [D, 512], BF16)
        dbg_lg = p.dout("dbg_lg", [E, LH])
        dbg_ltm = p.dout("dbg_ltm", [128, 16 * E])
        dbg_gtm = p.dout("dbg_gtm", [128, 16 * E])
        dbg_G = p.dout("dbg_G", [128, LH])
        dbg_h = p.dout("dbg_h", [128, 512], BF16)

    T = 512
    NTC = LH // T
    ND = 8

    for sname in ["ld", "wld", "pe", "ve", "se", "st"]:
        p.sem(sname)
    p.init_fences()
    sy, gp, te, dv, sc = nc.sync, nc.gpsimd, nc.tensor, nc.vector, nc.scalar

    acc = p.sbuf("acc", [128, ND * LH], F32)
    xn3 = p.sbuf("xn3", [128, ND * LH], BF16)
    hbuf = p.sbuf("hbuf", [128, 16 * LH], BF16)
    tmp_c = p.sbuf("tmp_c", [128, ND * T], F32)
    G_sb = p.sbuf("G_sb", [128, LH], F32)
    wbuf = [p.sbuf(f"wbuf{i}", [128, 16 * 128], BF16) for i in range(2)]
    wrt_s = p.sbuf("wrt_s", [128, ND * E], BF16)
    esel_s = p.sbuf("esel_s", [E, E * 128], F32)
    ident_s = p.sbuf("ident_s", [128, 128], F32)
    grow_d = nc.dram_tensor("grow_d", [E, LH], F32, kind="Internal").ap()
    grow = p.sbuf("grow", [E, 128], F32)
    grow_sb = p.sbuf("grow_sb", [E, T], F32)
    lg = G_sb[0:8, :]
    s_sc = p.sbuf("s_sc", [128, 2 * T], F32)
    rowbuf = s_sc[0:1, 0:2 * T]
    ltm = s_sc[:, 0:128]
    gtm = s_sc[:, 128:256]
    rb4 = s_sc[:, 256:384]
    rb1 = s_sc[:, 384:400]
    rb2 = s_sc[:, 400:416]
    rb3 = s_sc[:, 416:432]
    ones_col = p.sbuf("ones_col", [128, 1], F32)
    ones_row = p.sbuf("ones_row", [1, 128], F32)
    neg_half = p.sbuf("neg_half", [128, 1], F32)

    ps_lg = p.psum("ps_lg", [E, T])
    ps_t8 = p.psum("ps_t8", [128, 128])
    ps_G = [p.psum("ps_G0", [128, T]), p.psum("ps_G1", [128, T])]
    ps_h = [p.psum("ps_h0", [128, T]), p.psum("ps_h1", [128, T])]
    ps_y = [p.psum("ps_y0", [128, T]), p.psum("ps_y1", [128, T])]
    lnv_t = ps_h[0]
    rstd_t = ps_h[1]

    p.inc(gp.dma_start(out=wrt_s, in_=wrt), "ld", 16)
    p.inc(sy.dma_start(out=esel_s, in_=esel8), "ld", 16)
    p.inc(sy.dma_start(out=ident_s, in_=id_i), "ld", 16)
    LD0 = p.cnt("ld")
    p.wait(dv, "ld", LD0)
    p.inc(dv.memset(ones_col, 1.0), "ve")
    p.inc(dv.memset(ones_row, 1.0), "ve")
    p.inc(dv.memset(neg_half, -0.5), "ve")
    p.fence_dv()

    g = dict(pslg_se=0, psg_se=[0, 0], psh_se=[0, 0], psy_dv=[0, 0],
             w_pe=[0, 0], hb_pe=0, acc_dv=0, t8_se=0)

    for tc in range(NTC):
        t0 = tc * T
        p.wait(sy, "ve", g.get("init_ve", 0))
        p.wait(sy, "pe", g.get("init_pe", 0))
        for k in range(ND):
            p.inc(sy.dma_start(out=acc[:, k * LH + t0:k * LH + t0 + T],
                               in_=x2_i[k * 128:(k + 1) * 128, t0:t0 + T]),
                  "ld", 16)
            p.inc(sy.dma_start(out=tmp_c[:, k * T:(k + 1) * T],
                               in_=at_i[k * 128:(k + 1) * 128, t0:t0 + T]),
                  "ld", 16)
        LDC = p.cnt("ld")
        p.wait(dv, "ld", LDC)
        p.stall("dv")
        for k in range(ND):
            ins = dv.tensor_add(acc[:, k * LH + t0:k * LH + t0 + T],
                                acc[:, k * LH + t0:k * LH + t0 + T],
                                tmp_c[:, k * T:(k + 1) * T])
        p.inc(ins, "ve")
        p.fence_dv()
        x_t = [acc[:, k * LH + t0:k * LH + t0 + T] for k in range(ND)]
        x2_t = [tmp_c[:, k * T:(k + 1) * T] for k in range(ND)]
        xn_t = [xn3[:, k * LH + t0:k * LH + t0 + T] for k in range(ND)]
        emit_ln(p, te, sc, dv, x_t, x2_t, xn_t, ND, T,
                ones_col, ones_row, rowbuf, T, ps_y[0], ps_y[1], ps_G[0],
                ps_G[1], neg_half, lnv_t, rstd_t,
                ms_row=G_sb[0:1, 0:T])
        g["init_ve"] = p.cnt("ve")
        g["init_pe"] = p.cnt("pe")
    p.fence_dv()
    XN3 = p.cnt("ve")

    p.wait(te, "ve", XN3)
    p.wait(te, "ld", LD0)
    for tc in range(NTC):
        t0 = tc * T
        p.wait(te, "se", g["pslg_se"])
        for k in range(ND):
            ins = te.matmul(ps_lg, wrt_s[:, k * E:(k + 1) * E],
                            xn3[:, k * LH + t0:k * LH + t0 + T],
                            start=(k == 0), stop=(k == ND - 1))
        p.inc(ins, "pe")
        p.wait(sc, "pe", p.cnt("pe"))
        p.inc(sc.activation(lg[:, t0:t0 + T], ps_lg, AF.Identity), "se")
        g["pslg_se"] = p.cnt("se")
    p.fence_sc()

    p.wait(te, "se", p.cnt("se"))
    for sub in range(16):
        p.wait(te, "se", g["t8_se"])
        ins = te.transpose(ps_t8[:, 0:8], lg[:, sub * 128:(sub + 1) * 128],
                           ident_s[0:8, 0:8])
        p.inc(ins, "pe")
        p.wait(sc, "pe", p.cnt("pe"))
        p.inc(sc.activation(ltm[:, sub * E:(sub + 1) * E], ps_t8[:, 0:8],
                            AF.Identity), "se")
        g["t8_se"] = p.cnt("se")
    p.fence_sc()
    p.wait(dv, "se", p.cnt("se"))
    p.stall("dv")

    for sub in range(16):
        dv.tensor_reduce(rb1[:, sub:sub + 1], ltm[:, sub * E:(sub + 1) * E],
                         AX.X, ALU.max, negate=True)
    for sub in range(16):
        dv.tensor_scalar(rb4[:, sub * E:(sub + 1) * E],
                         ltm[:, sub * E:(sub + 1) * E], -1.0, rb1[:, sub:sub + 1],
                         ALU.mult, ALU.is_equal)
    for sub in range(16):
        dv.scalar_tensor_tensor(rb4[:, sub * E:(sub + 1) * E],
                                rb4[:, sub * E:(sub + 1) * E], -1e30,
                                ltm[:, sub * E:(sub + 1) * E],
                                ALU.mult, ALU.add)
    for sub in range(16):
        ins = dv.tensor_reduce(rb2[:, sub:sub + 1],
                               rb4[:, sub * E:(sub + 1) * E], AX.X, ALU.max)
    p.inc(ins, "ve")
    p.fence_dv()
    p.wait(sc, "ve", p.cnt("ve"))
    p.stall("sc")
    for sub in range(16):
        sc.activation(gtm[:, sub * E:(sub + 1) * E],
                      ltm[:, sub * E:(sub + 1) * E], AF.Exp,
                      bias=rb1[:, sub:sub + 1])
    for sub in range(16):
        ins = sc.activation(rb3[:, sub:sub + 1], rb2[:, sub:sub + 1], AF.Exp,
                            bias=rb1[:, sub:sub + 1])
    p.inc(ins, "se")
    p.fence_sc()
    p.wait(dv, "se", p.cnt("se"))
    p.stall("dv")
    for sub in range(16):
        dv.tensor_scalar_add(rb3[:, sub:sub + 1], rb3[:, sub:sub + 1], 1.0)
    for sub in range(16):
        dv.reciprocal(rb3[:, sub:sub + 1], rb3[:, sub:sub + 1])
    for sub in range(16):
        dv.tensor_single_scalar(rb4[:, sub * E:(sub + 1) * E],
                                ltm[:, sub * E:(sub + 1) * E],
                                rb2[:, sub:sub + 1], ALU.is_ge)
    for sub in range(16):
        dv.tensor_mul(gtm[:, sub * E:(sub + 1) * E],
                      gtm[:, sub * E:(sub + 1) * E],
                      rb4[:, sub * E:(sub + 1) * E])
    for sub in range(16):
        ins = dv.tensor_scalar_mul(gtm[:, sub * E:(sub + 1) * E],
                                   gtm[:, sub * E:(sub + 1) * E],
                                   rb3[:, sub:sub + 1])
    p.inc(ins, "ve")
    p.fence_dv()
    p.fence_dv()

    p.wait(te, "ve", p.cnt("ve"))
    for sub in range(16):
        p.wait(te, "se", g["t8_se"])
        ins = te.transpose(ps_t8[0:8, :], gtm[:, sub * E:(sub + 1) * E],
                           ident_s)
        p.inc(ins, "pe")
        p.wait(sc, "pe", p.cnt("pe"))
        p.wait(sc, "st", g.get("grow_st", 0))
        p.inc(sc.activation(grow, ps_t8[0:8, 0:128], AF.Identity), "se")
        g["t8_se"] = p.cnt("se")
        p.fence_sc()
        p.wait(sy, "se", p.cnt("se"))
        p.inc(sy.dma_start(out=grow_d[:, sub * 128:(sub + 1) * 128],
                           in_=grow), "st", 16)
        g["grow_st"] = p.cnt("st")
    GROW_ST = p.cnt("st")

    if dbg:
        p.wait(sy, "se", p.cnt("se"))
        p.wait(sy, "ve", p.cnt("ve"))
        for k in range(ND):
            p.inc(sy.dma_start(out=dbg_xn3[k * 128:(k + 1) * 128, :],
                               in_=xn3[:, k * LH:k * LH + 512]), "st", 16)
        p.inc(sy.dma_start(out=dbg_lg, in_=lg), "st", 16)
        p.inc(sy.dma_start(out=dbg_ltm, in_=ltm.tensor.ap()[:, 0:128]), "st", 16)
        p.inc(sy.dma_start(out=dbg_gtm, in_=gtm.tensor.ap()[:, 128:256]), "st", 16)
        DBG1 = p.cnt("st")
        p.wait(sc, "st", DBG1)
        p.wait(dv, "st", DBG1)

    for e in range(n_experts):
        for tc in range(NTC):
            t0 = tc * T
            pb = tc % 2
            p.wait(sy, "st", GROW_ST)
            p.wait(sy, "pe", g.get("growsb_pe", 0))
            p.inc(sy.dma_start(out=grow_sb, in_=grow_d[:, t0:t0 + T]),
                  "ld", 16)
            p.wait(te, "ld", p.cnt("ld"))
            p.wait(te, "se", g["psg_se"][pb])
            ins = te.matmul(ps_G[pb], esel_s[:, e * 128:(e + 1) * 128],
                            grow_sb, start=True, stop=True)
            p.inc(ins, "pe")
            g["growsb_pe"] = p.cnt("pe")
            p.wait(sc, "pe", p.cnt("pe"))
            p.wait(sc, "ve", g["acc_dv"])
            p.inc(sc.activation(G_sb[:, t0:t0 + T], ps_G[pb], AF.Identity),
                  "se")
            g["psg_se"][pb] = p.cnt("se")
        p.fence_sc()
        GSE = p.cnt("se")

        for half in range(2):
            for oi in range(16):
                ot = half * 16 + oi
                wb = wbuf[oi % 2]
                p.wait(gp, "pe", g["w_pe"][oi % 2])
                p.inc(gp.dma_start(out=wb[:, 0:8 * 128],
                                   in_=w1T[e, ot]), "wld", 16)
                WLD = p.cnt("wld")
                p.wait(te, "wld", WLD)
                for tc in range(NTC):
                    t0 = tc * T
                    ph = ps_h[tc % 2]
                    p.wait(te, "se", g["psh_se"][tc % 2])
                    if e == 0 and half == 0 and oi == 0:
                        p.wait(te, "ve", XN3)
                    for k in range(ND):
                        ins = te.matmul(ph, wb[:, k * 128:(k + 1) * 128],
                                        xn3[:, k * LH + t0:k * LH + t0 + T],
                                        start=(k == 0), stop=(k == ND - 1))
                    p.inc(ins, "pe")
                    g["w_pe"][oi % 2] = p.cnt("pe")
                    p.wait(sc, "pe", p.cnt("pe"))
                    p.wait(sc, "pe", g["hb_pe"])
                    p.inc(sc.activation(hbuf[:, oi * LH + t0:oi * LH + t0 + T],
                                        ph, AF.Silu), "se")
                    g["psh_se"][tc % 2] = p.cnt("se")
            p.fence_sc()
            HSE = p.cnt("se")
            for dt in range(ND):
                wb = wbuf[dt % 2]
                p.wait(gp, "pe", g["w_pe"][dt % 2])
                p.inc(gp.dma_start(
                    out=wb[:, 0:16 * 128],
                    in_=w2T[e, dt, :, half * 16 * 128:(half + 1) * 16 * 128]),
                    "wld", 16)
                WLD = p.cnt("wld")
                p.wait(te, "wld", WLD)
                p.wait(te, "se", HSE)
                for tc in range(NTC):
                    t0 = tc * T
                    py = ps_y[tc % 2]
                    p.wait(te, "ve", g["psy_dv"][tc % 2])
                    for k in range(16):
                        ins = te.matmul(py, wb[:, k * 128:(k + 1) * 128],
                                        hbuf[:, k * LH + t0:k * LH + t0 + T],
                                        start=(k == 0), stop=(k == 15))
                    p.inc(ins, "pe")
                    g["w_pe"][dt % 2] = p.cnt("pe")
                    PEY = p.cnt("pe")
                    p.wait(dv, "pe", PEY)
                    p.wait(dv, "se", GSE)
                    dv.tensor_mul(tmp_c[:, dt * T:(dt + 1) * T], py,
                                  G_sb[:, t0:t0 + T])
                    ins = dv.tensor_add(acc[:, dt * LH + t0:dt * LH + t0 + T],
                                        acc[:, dt * LH + t0:dt * LH + t0 + T],
                                        tmp_c[:, dt * T:(dt + 1) * T])
                    p.inc(ins, "ve")
                    g["psy_dv"][tc % 2] = p.cnt("ve")
            g["hb_pe"] = p.cnt("pe")
            if dbg and e == 0 and half == 0:
                p.wait(sy, "se", p.cnt("se"))
                p.inc(sy.dma_start(out=dbg_h, in_=hbuf[:, 0:512]), "st", 16)
                p.inc(sy.dma_start(out=dbg_G, in_=G_sb), "st", 16)
                DBG2 = p.cnt("st")
                p.wait(sc, "st", DBG2)
                p.wait(te, "st", DBG2)
        g["acc_dv"] = p.cnt("ve")
    p.fence_dv()

    p.wait(sy, "ve", p.cnt("ve"))
    for k in range(ND):
        p.inc(sy.dma_start(out=out_o[k * 128:(k + 1) * 128, :],
                           in_=acc[:, k * LH:(k + 1) * LH]), "st", 16)
    p.wait(sy, "st")
    p.close()
    return p


def prep_L5(inputs, l3_results, l4_results):
    W1 = np.asarray(inputs["L1_ffn_W1"], np.float32)
    W2 = np.asarray(inputs["L1_ffn_W2"], np.float32)
    Wr = np.asarray(inputs["L1_ffn_Wr"], np.float32)
    w1T = np.stack([tilize(W1[e].T, 8, 32) for e in range(E)])
    w2T = np.stack([tilize(W2[e].T, 32, 8) for e in range(E)])
    wrt = np.ascontiguousarray(
        Wr.T.reshape(8, 128, E).transpose(1, 0, 2).reshape(128, 8 * E))
    esel8 = np.zeros((E, E * 128), np.float32)
    for e in range(E):
        esel8[e, e * 128:(e + 1) * 128] = 1.0
    at_full = []
    for b in range(B):
        a = np.zeros((D, L), np.float32)
        for par in range(2):
            got = np.asarray(l4_results[2 * b + par]["attn_out"])
            for qi in range(16):
                qg = 2 * qi + par
                a[:, qg * 128:(qg + 1) * 128] = got[qi]
        at_full.append(a)
    in_maps = []
    for c in range(8):
        b, h = c // 2, c % 2
        sl = slice(h * LH, (h + 1) * LH)
        in_maps.append(dict(
            x2_i=np.ascontiguousarray(np.asarray(l3_results[c]["x2"])),
            at_i=np.ascontiguousarray(at_full[b][:, sl]),
            w1T=w1T, w2T=w2T, wrt=wrt, esel8=esel8,
            id_i=np.eye(128, dtype=np.float32)))
    return in_maps




def kernel(**inputs):
    cores = list(range(8))
    p1 = build_L1()
    r1 = run_bass_kernel_spmd(p1.nc, prep_L1(inputs), core_ids=cores)
    p1.close if False else None
    p2 = build_L2()
    r2 = run_bass_kernel_spmd(p2.nc, prep_L2(inputs, r1.results),
                              core_ids=cores)
    p3 = build_L3()
    r3 = run_bass_kernel_spmd(p3.nc, prep_L3(inputs, r2.results),
                              core_ids=cores)
    p4 = build_L4()
    r4 = run_bass_kernel_spmd(p4.nc, prep_L4(inputs, r3.results),
                              core_ids=cores)
    p5 = build_L5()
    r5 = run_bass_kernel_spmd(p5.nc, prep_L5(inputs, r3.results, r4.results),
                              core_ids=cores)
    out = np.empty((B, L, D), np.float32)
    for c in range(8):
        b, h = c // 2, c % 2
        out[b, h * LH:(h + 1) * LH, :] = np.asarray(
            r5.results[c]["out_fm"]).T
    return out
